# version 17 (final)
# baseline (speedup 1.0000x reference)
import contextlib
import numpy as np
import ml_dtypes
import concourse.bass as bass
import concourse.mybir as mybir
from concourse.bass_utils import run_bass_kernel_spmd

F32 = mybir.dt.float32
BF16 = mybir.dt.bfloat16
AF = mybir.ActivationFunctionType
ALU = mybir.AluOpType

D = 2048
N = 2048
CH = D // 128
H = 16
NI = 5632
JI = NI // 128
NSTRIP = 4
SW = 512
KT = N // 128
EPS_LN = 1e-5
EPS_ATT = 1e-6

ENGS = ("sync", "gpsimd", "scalar", "vector", "tensor")


class Sched:

    def __init__(self):
        self.q = {e: [] for e in ENGS}
        self.cnt = {}
        self.waited = {e: {} for e in ENGS}
        self.names = []
        self.last_comp = {e: None for e in ENGS}

    def sem(self, name):
        if name not in self.cnt:
            self.cnt[name] = 0
            self.names.append(name)
        return name

    def emit(self, eng, fn, waits=(), inc=None, n=1):
        waits = list(waits)
        if eng != "tensor" and (inc in ("CP", "CA", "CD", "CG") or inc is None):
            waits.append(self.last_comp[eng])
        w = []
        for t in waits:
            if t is None:
                continue
            s, v = t
            assert v <= self.cnt[s], ("wait-on-future", s, v, self.cnt[s])
            if v > self.waited[eng].get(s, 0):
                self.waited[eng][s] = v
                w.append((s, v))
        tok = None
        if inc is not None:
            self.cnt[inc] += n
            tok = (inc, self.cnt[inc])
            if inc in ("CP", "CA", "CD", "CG"):
                self.last_comp[eng] = tok
        self.q[eng].append((w, fn, inc, n))
        return tok

    def dma(self, eng, sem, out, in_, waits=(), transpose=False):
        if transpose:
            fn = lambda e, o=out, i=in_: e.dma_start(out=o, in_=i, transpose=True)
        else:
            fn = lambda e, o=out, i=in_: e.dma_start(out=o, in_=i)
        return self.emit(eng, fn, waits, inc=sem, n=16)

    def bar(self):
        return [("CP", self.cnt["CP"]), ("CA", self.cnt["CA"]),
                ("CD", self.cnt["CD"]), ("CG", self.cnt["CG"])]


class Bank:
    __slots__ = ("ap", "wtok", "rtok")

    def __init__(self, ap):
        self.ap = ap
        self.wtok = None
        self.rtok = None


def build_nc(debug=False):
    nc = bass.Bass("TRN2", target_bir_lowering=False)
    S = Sched()
    for s in ("LDA", "LDB", "LDX", "STA", "STB", "W0", "W1", "VT0", "VT1",
              "CP", "CA", "CD", "CG"):
        S.sem(s)
    LD = ("LDA", "LDB")
    ST = ("STA", "STB")
    WS = ("W0", "W1")
    VT = ("VT0", "VT1")

    xT = nc.dram_tensor("xT", [D, N], F32, kind="ExternalInput").ap()
    wq = nc.dram_tensor("wq_hl", [CH, 128, CH, 128], F32, kind="ExternalInput").ap()
    wk = nc.dram_tensor("wk_hl", [CH, 128, CH, 128], F32, kind="ExternalInput").ap()
    wv = nc.dram_tensor("wv_hl", [CH, 128, CH, 128], F32, kind="ExternalInput").ap()
    wo = nc.dram_tensor("wo_hl", [CH, 128, CH, 128], F32, kind="ExternalInput").ap()
    gw = nc.dram_tensor("gate_hl", [JI, 128, CH, 128], F32, kind="ExternalInput").ap()
    uw = nc.dram_tensor("up_hl", [JI, 128, CH, 128], F32, kind="ExternalInput").ap()
    dw = nc.dram_tensor("down_hl", [CH, 128, JI, 128], F32, kind="ExternalInput").ap()
    gs = nc.dram_tensor("gate_s_t", [128, JI], F32, kind="ExternalInput").ap()
    us = nc.dram_tensor("up_s_t", [128, JI], F32, kind="ExternalInput").ap()
    ds = nc.dram_tensor("down_s_t", [128, CH], F32, kind="ExternalInput").ap()
    cs = nc.dram_tensor("cossin", [128, N], F32, kind="ExternalInput").ap()
    mk = nc.dram_tensor("maskv", [128, 128], F32, kind="ExternalInput").ap()
    outT = nc.dram_tensor("outT", [D, N], F32, kind="ExternalOutput").ap()

    ksc = "ExternalOutput" if debug else "Internal"
    xn_sc = nc.dram_tensor("xn_sc", [CH, 128, N], BF16, kind=ksc).ap()
    q_sc = nc.dram_tensor("q_sc", [CH, 128, N], BF16, kind=ksc).ap()
    v_sc = nc.dram_tensor("v_sc", [D, N], BF16, kind=ksc).ap()
    o_sc = nc.dram_tensor("o_sc", [CH, 128, N], BF16, kind=ksc).ap()
    s_sc = nc.dram_tensor("s_sc", [CH, 128, N], F32, kind=ksc).ap()

    xT_r = xT.rearrange("(i p) t -> i p t", p=128)
    xn_sc_r = xn_sc.rearrange("i p t -> p i t")

    with contextlib.ExitStack() as ctx:
        E = ctx.enter_context
        wslot = [E(nc.sbuf_tensor(f"wslot{i}", [128, NI], BF16)) for i in range(2)]
        ones_f = E(nc.sbuf_tensor("ones_f", [128, 1], F32))
        ones_b = E(nc.sbuf_tensor("ones_b", [128, 1], BF16))
        onesM = E(nc.sbuf_tensor("onesM", [128, 128], BF16))
        negone = E(nc.sbuf_tensor("negone", [128, 1], F32))
        t_ones = S.emit("vector", lambda e: (e.memset(ones_f[:], 1.0),
                                             e.memset(ones_b[:], 1.0),
                                             e.memset(onesM[:], 1.0),
                                             e.memset(negone[:], -1.0))[-1],
                        inc="CD")
        wcons = [None, None]
        wc = [0]

        def wload(dram_ap, cols, ncols_i):
            par = wc[0] % 2
            view = wslot[par][:, cols[0]:cols[1]].rearrange(
                "p (i n) -> p i n", i=ncols_i)
            tok = S.dma("gpsimd", WS[par], view, dram_ap, waits=[wcons[par]])
            return par, tok

        def wdone(par, tok):
            wcons[par] = tok
            wc[0] += 1

        xn_ready = []
        xn_spill = []
        k_ready = []

        with contextlib.ExitStack() as c13:
            E13 = c13.enter_context
            K_sb = E13(nc.sbuf_tensor("K_sb", [128, H * N], BF16))
            with contextlib.ExitStack() as c12:
                E12 = c12.enter_context
                cossin = E12(nc.sbuf_tensor("cossin_sb", [128, N], F32))
                ones_r = E12(nc.sbuf_tensor("ones_r", [1, 128], F32))
                eps_sb = E12(nc.sbuf_tensor("eps_sb", [1, 1], F32))
                t_ones2 = S.emit("vector",
                                 lambda e: (e.memset(ones_r[:], 1.0),
                                            e.memset(eps_sb[:], EPS_LN))[-1],
                                 inc="CD")
                t_cs = S.dma("sync", "LDX", cossin[:], cs[:, :])
                xn_sb = E12(nc.sbuf_tensor("xn_sb", [128, CH * N], BF16))

                with contextlib.ExitStack() as c1:
                    E1 = c1.enter_context
                    xa = [E1(nc.sbuf_tensor(f"xa{i}", [128, 4 * SW], F32))
                          for i in range(2)]
                    xb = [E1(nc.sbuf_tensor(f"xb{i}", [128, 4 * SW], F32))
                          for i in range(2)]
                    sq = [E1(nc.sbuf_tensor(f"sq{i}", [128, SW], F32))
                          for i in range(2)]
                    tb = [E1(nc.sbuf_tensor(f"tb{i}", [128, SW], F32))
                          for i in range(2)]
                    stat = [E1(nc.sbuf_tensor("stat0", [1, 4 * SW], F32))] * 2
                    ps_sum = [Bank(E1(nc.psum_tensor(f"ps_sum{i}", [1, SW], F32)))
                              for i in range(2)]
                    ps_sq = [Bank(E1(nc.psum_tensor(f"ps_sq{i}", [1, SW], F32)))
                             for i in range(2)]
                    ps_mb = [Bank(E1(nc.psum_tensor(f"ps_mb{i}", [128, SW], F32)))
                             for i in range(2)]
                    ps_rb = [Bank(E1(nc.psum_tensor(f"ps_rb{i}", [128, SW], F32)))
                             for i in range(2)]
                    xT_g = xT.rearrange("(g c p) t -> g p c t", g=4, p=128)

                    xa_cons = [None, None]
                    xa_cons_a = [None, None]
                    xb_cons = [None, None]
                    sq_cons = [None, None]
                    tb_cons = [None, None]
                    rb_tok = [None, None]
                    pe_stats_tok = [None, None]

                    def p1_pass1(s):
                        sp = s % 2
                        pss, psq = ps_sum[sp], ps_sq[sp]
                        for g in range(4):
                            p = g % 2
                            lt = S.dma(
                                "sync", LD[p],
                                xa[p][:].rearrange("q (c t) -> q c t", c=4),
                                xT_g[g][:, :, s * SW:(s + 1) * SW],
                                waits=[xa_cons[p], xa_cons_a[p]])
                            for c in range(4):
                                i = 4 * g + c
                                pq = i % 2
                                xv = xa[p][:, c * SW:(c + 1) * SW]
                                tsq = S.emit(
                                    "scalar",
                                    lambda e, pq=pq, xv=xv: e.activation(
                                        out=sq[pq][:], in_=xv, func=AF.Square),
                                    [lt, sq_cons[pq]], inc="CA")
                                last = S.emit(
                                    "tensor",
                                    lambda e, xv=xv, pss=pss,
                                           s0=(i == 0), s1=(i == CH - 1):
                                        e.matmul(pss.ap[:], ones_f[:], xv,
                                                 start=s0, stop=s1),
                                    [lt, t_ones] +
                                    ([pss.rtok, psq.rtok] if i == 0 else []),
                                    inc="CP")
                                if c == 3:
                                    xa_cons[p] = last
                                    xa_cons_a[p] = tsq
                                sq_cons[pq] = S.emit(
                                    "tensor",
                                    lambda e, pq=pq, psq=psq,
                                           s0=(i == 0), s1=(i == CH - 1):
                                        e.matmul(psq.ap[:], ones_f[:], sq[pq][:],
                                                 start=s0, stop=s1),
                                    [tsq], inc="CP")
                        pe_stats_tok[sp] = sq_cons[(CH - 1) % 2]

                    def p1_stats(s):
                        sp = s % 2
                        pss, psq = ps_sum[sp], ps_sq[sp]
                        mean = stat[sp][0:1, 0:SW]
                        ex2 = stat[sp][0:1, SW:2 * SW]
                        m2 = stat[sp][0:1, 2 * SW:3 * SW]
                        rstd = stat[sp][0:1, 3 * SW:4 * SW]
                        t_mean = S.emit(
                            "vector",
                            lambda e, mean=mean, pss=pss: e.tensor_scalar_mul(
                                mean, pss.ap[:], 1.0 / D),
                            [pe_stats_tok[sp]], inc="CD")
                        pss.rtok = t_mean
                        t_ex2 = S.emit(
                            "vector",
                            lambda e, ex2=ex2, psq=psq: e.tensor_scalar_mul(
                                ex2, psq.ap[:], 1.0 / D),
                            [], inc="CD")
                        psq.rtok = t_ex2
                        S.emit("vector",
                               lambda e, m2=m2, mean=mean: e.tensor_mul(
                                   m2, mean, mean), [], inc="CD")
                        t_var = S.emit("vector",
                                       lambda e, ex2=ex2, m2=m2: e.tensor_sub(
                                           ex2, ex2, m2), [], inc="CD")
                        t_sd = S.emit(
                            "scalar",
                            lambda e, m2=m2, ex2=ex2: e.activation(
                                out=m2, in_=ex2, func=AF.Sqrt,
                                bias=eps_sb[0:1, 0:1]),
                            [t_var, t_ones2], inc="CA")
                        t_rstd = S.emit("vector",
                                        lambda e, rstd=rstd, m2=m2:
                                            e.reciprocal(rstd, m2),
                                        [t_sd], inc="CD")
                        t_mb = S.emit(
                            "tensor",
                            lambda e, sp=sp, mean=mean: e.matmul(
                                ps_mb[sp].ap[:], ones_r[:], mean,
                                start=True, stop=True),
                            [t_mean, t_ones2, ps_mb[sp].rtok], inc="CP")
                        t_rb = S.emit(
                            "tensor",
                            lambda e, sp=sp, rstd=rstd: e.matmul(
                                ps_rb[sp].ap[:], ones_r[:], rstd,
                                start=True, stop=True),
                            [t_rstd, ps_rb[sp].rtok], inc="CP")
                        rb_tok[sp] = (t_mb, t_rb)

                    def p1_pass2(s):
                        sp = s % 2
                        t_mb, t_rb = rb_tok[sp]
                        t_norm = None
                        for g in range(4):
                            p = g % 2
                            lt = S.dma(
                                "sync", VT[p],
                                xb[p][:].rearrange("q (c t) -> q c t", c=4),
                                xT_g[g][:, :, s * SW:(s + 1) * SW],
                                waits=[xb_cons[p]])
                            for c in range(4):
                                i = 4 * g + c
                                pq = i % 2
                                xv = xb[p][:, c * SW:(c + 1) * SW]
                                S.emit(
                                    "vector",
                                    lambda e, pq=pq, xv=xv, sp=sp: e.tensor_sub(
                                        tb[pq][:], xv, ps_mb[sp].ap[:]),
                                    [lt, t_mb, tb_cons[pq]], inc="CD")
                                t_norm = S.emit(
                                    "vector",
                                    lambda e, pq=pq, i=i, s=s, sp=sp: e.tensor_mul(
                                        xn_sb[:, i * N + s * SW:
                                              i * N + (s + 1) * SW],
                                        tb[pq][:], ps_rb[sp].ap[:]),
                                    [t_rb], inc="CD")
                                tb_cons[pq] = t_norm
                                if c == 3:
                                    xb_cons[p] = t_norm
                        ps_mb[sp].rtok = t_norm
                        ps_rb[sp].rtok = t_norm
                        xn_ready.append(t_norm)
                        xn_view = xn_sb[:].rearrange("p (i t) -> p i t", i=CH)
                        xn_spill.append(S.dma(
                            "sync", ST[s % 2],
                            xn_sc_r[:, :, s * SW:(s + 1) * SW],
                            xn_view[:, :, s * SW:(s + 1) * SW],
                            waits=[t_norm]))

                    for s in range(NSTRIP + 1):
                        if s < NSTRIP:
                            p1_pass1(s)
                            p1_stats(s)
                        if s >= 1:
                            p1_pass2(s - 1)

                with contextlib.ExitStack() as c2:
                    E2 = c2.enter_context
                    bar12 = S.bar()
                    qstage = [E2(nc.sbuf_tensor(f"qstage{i}", [128, N], BF16))
                              for i in range(2)]
                    rA1 = [E2(nc.sbuf_tensor(f"rA1_{i}", [64, SW], F32))
                           for i in range(2)]
                    rA2 = [E2(nc.sbuf_tensor(f"rA2_{i}", [64, SW], F32))
                           for i in range(2)]
                    rB1 = [E2(nc.sbuf_tensor(f"rB1_{i}", [64, SW], F32))
                           for i in range(2)]
                    rB2 = [E2(nc.sbuf_tensor(f"rB2_{i}", [64, SW], F32))
                           for i in range(2)]
                    pool_cons = [None, None]
                    grp = [[Bank(E2(nc.psum_tensor(f"pj{g}_{s}", [128, SW], F32)))
                            for s in range(4)] for g in range(2)]
                    stage_cons = [None, None]

                    def proj(wap, dest):
                        last_evac = None
                        for j in range(CH):
                            par, wtok = wload(wap[j], (0, D), CH)
                            banks = grp[j % 2]
                            mmtok = None
                            for i in range(CH):
                                for st in range(4):
                                    b = banks[st]
                                    w = ([wtok] + xn_ready + bar12
                                         if (i == 0 and st == 0) else [])
                                    if i == 0:
                                        w = w + [b.rtok]
                                    mmtok = S.emit(
                                        "tensor",
                                        lambda e, b=b, par=par, i=i, st=st,
                                               s0=(i == 0), s1=(i == CH - 1):
                                            e.matmul(
                                                b.ap[:],
                                                wslot[par][:, i * 128:(i + 1) * 128],
                                                xn_sb[:, i * N + st * SW:
                                                      i * N + (st + 1) * SW],
                                                start=s0, stop=s1),
                                        w, inc="CP" if i == CH - 1 else None)
                                    if i == CH - 1:
                                        b.wtok = mmtok
                            wdone(par, mmtok)
                            for st in range(4):
                                b = banks[st]
                                if dest == "v":
                                    tok = S.emit(
                                        "scalar",
                                        lambda e, b=b, j=j, st=st: e.activation(
                                            out=qstage[j % 2][:,
                                                              st * SW:(st + 1) * SW],
                                            in_=b.ap[:], func=AF.Identity),
                                        [b.wtok, stage_cons[j % 2]] + bar12,
                                        inc="CA")
                                    b.rtok = tok
                                    last_evac = tok
                                else:
                                    co = cossin[0:64, st * SW:(st + 1) * SW]
                                    si = cossin[64:128, st * SW:(st + 1) * SW]
                                    if dest == "k":
                                        d0 = K_sb[0:64,
                                                  j * N + st * SW: j * N + (st + 1) * SW]
                                        d1 = K_sb[64:128,
                                                  j * N + st * SW: j * N + (st + 1) * SW]
                                        war = []
                                    else:
                                        d0 = qstage[j % 2][0:64, st * SW:(st + 1) * SW]
                                        d1 = qstage[j % 2][64:128, st * SW:(st + 1) * SW]
                                        war = [stage_cons[j % 2]]
                                    rp = (j * 4 + st) % 2
                                    S.emit(
                                        "vector",
                                        lambda e, b=b, co=co, rp=rp: e.tensor_mul(
                                            rA1[rp][:], b.ap[0:64, :], co),
                                        [b.wtok, t_cs] + war + bar12, inc="CD")
                                    S.emit(
                                        "vector",
                                        lambda e, b=b, si=si, rp=rp: e.tensor_mul(
                                            rA2[rp][:], b.ap[64:128, :], si),
                                        [], inc="CD")
                                    S.emit(
                                        "vector",
                                        lambda e, d0=d0, rp=rp: e.tensor_sub(
                                            d0, rA1[rp][:], rA2[rp][:]),
                                        [], inc="CD")
                                    S.emit(
                                        "vector",
                                        lambda e, b=b, si=si, rp=rp: e.tensor_mul(
                                            rB1[rp][:], b.ap[0:64, :], si),
                                        [], inc="CD")
                                    tok = S.emit(
                                        "vector",
                                        lambda e, b=b, co=co, rp=rp: e.tensor_mul(
                                            rB2[rp][:], b.ap[64:128, :], co),
                                        [], inc="CD")
                                    b.rtok = tok
                                    last_evac = S.emit(
                                        "vector",
                                        lambda e, d1=d1, rp=rp: e.tensor_add(
                                            d1, rB1[rp][:], rB2[rp][:]),
                                        [], inc="CD")
                            if dest == "q":
                                stage_cons[j % 2] = S.dma(
                                    "sync", ST[j % 2], q_sc[j, :, :],
                                    qstage[j % 2][:], waits=[last_evac])
                            elif dest == "v":
                                stage_cons[j % 2] = S.dma(
                                    "sync", ST[j % 2],
                                    v_sc[j * 128:(j + 1) * 128, :],
                                    qstage[j % 2][:], waits=[last_evac])
                            else:
                                k_ready.append(last_evac)
                        return last_evac

                    proj(wq, "q")
                    proj(wk, "k")
                    v_last = proj(wv, "v")
                    qv_store_done = [("STA", S.cnt["STA"]), ("STB", S.cnt["STB"])]

            with contextlib.ExitStack() as c3:
                E3 = c3.enter_context
                bar23 = S.bar()
                masks = E3(nc.sbuf_tensor("masks", [128, 128], F32))
                t_mk = S.dma("sync", "LDX", masks[:], mk[:, :],
                             waits=bar23 + qv_store_done)
                Qs = [E3(nc.sbuf_tensor(f"Qs{i}", [128, H * SW], BF16))
                      for i in range(2)]
                V_sb = E3(nc.sbuf_tensor("V_sb", [128, KT * D], BF16))
                S4 = [[E3(nc.sbuf_tensor(f"S4_{a}_{b}", [128, SW], BF16))
                       for b in range(2)] for a in range(2)]
                T1 = [[E3(nc.sbuf_tensor(f"T1_{a}_{b}", [128, SW], F32))
                       for b in range(2)] for a in range(2)]
                rc1 = [E3(nc.sbuf_tensor(f"rc1_{i}", [128, SW], F32))
                       for i in range(2)]
                Obf = [E3(nc.sbuf_tensor(f"Obf{i}", [128, SW], BF16))
                       for i in range(2)]
                ps_s = [[Bank(E3(nc.psum_tensor(f"ps_s{a}_{b}", [128, SW], F32)))
                         for b in range(2)] for a in range(2)]
                ps_o = [Bank(E3(nc.psum_tensor(f"ps_o{i}", [128, SW], F32)))
                        for i in range(2)]
                ps_d = [Bank(E3(nc.psum_tensor(f"ps_d{i}", [128, SW], F32)))
                        for i in range(2)]

                kr = [k_ready[-1], v_last]
                qs_load_tok = [None] * NSTRIP
                qs_cons = [None, None]
                s4_cons = [[None] * 2 for _ in range(2)]
                t1_cons = [[None] * 2 for _ in range(2)]
                obf_st = [None, None]
                rc1_cons = [None, None]

                def load_qstrip(qs):
                    par = qs % 2
                    lt = None
                    for j in range(CH):
                        lt = S.dma("sync", LD[par],
                                   Qs[par][:, j * SW:(j + 1) * SW],
                                   q_sc[j, :, qs * SW:(qs + 1) * SW],
                                   waits=([qs_cons[par]] + qv_store_done + bar23)
                                   if j == 0 else [])
                    qs_load_tok[qs] = lt

                def load_vtile(k):
                    S.dma("sync", VT[k % 2], V_sb[:, k * D:(k + 1) * D],
                          v_sc[:, k * 128:(k + 1) * 128],
                          waits=qv_store_done + bar23, transpose=True)

                load_qstrip(0)
                for k in range(4):
                    load_vtile(k)

                pend = []

                def emit_finalize():
                    while pend:
                        h, qs, hsel, sv_stop, dn_stop = pend.pop(0)
                        bo, bd = ps_o[hsel], ps_d[hsel]
                        hp = h % 2
                        t_dm = S.emit(
                            "vector",
                            lambda e, bd=bd, hp=hp: e.tensor_scalar_max(
                                rc1[hp][:], bd.ap[:], EPS_ATT),
                            [dn_stop, rc1_cons[hp]], inc="CD")
                        bd.rtok = t_dm
                        t_ln = S.emit(
                            "scalar",
                            lambda e, hp=hp: e.activation(
                                out=rc1[hp][:], in_=rc1[hp][:], func=AF.Ln),
                            [t_dm], inc="CA")
                        t_ex = S.emit(
                            "scalar",
                            lambda e, hp=hp: e.activation(
                                out=rc1[hp][:], in_=rc1[hp][:], func=AF.Exp,
                                scale=negone[:, 0:1]),
                            [t_ln, t_ones], inc="CA")
                        t_ob = S.emit(
                            "vector",
                            lambda e, bo=bo, hp=hp: e.tensor_mul(
                                Obf[hp][:], bo.ap[:], rc1[hp][:]),
                            [t_ex, sv_stop, obf_st[hp]], inc="CD")
                        bo.rtok = t_ob
                        rc1_cons[hp] = t_ob
                        obf_st[hp] = S.dma(
                            "sync", ST[hp], o_sc[h, :, qs * SW:(qs + 1) * SW],
                            Obf[hp][:], waits=[t_ob])

                for qs in range(NSTRIP):
                    nk = 4 * qs + 4
                    vt_full = [("VT0", S.cnt["VT0"]), ("VT1", S.cnt["VT1"])]
                    if qs + 1 < NSTRIP:
                        load_qstrip(qs + 1)
                    for g in range(H // 2):
                        hh = (2 * g, 2 * g + 1)
                        qk_tok = [[None] * nk, [None] * nk]
                        sq_tok = [[None] * nk, [None] * nk]

                        def emit_qk(hsel, k, qs=qs, qk_tok=qk_tok, hh=hh):
                            kp = k % 2
                            b = ps_s[hsel][kp]
                            h = hh[hsel]
                            o = k - 4 * qs
                            off = max(o, 0) * 128
                            w = [qs_load_tok[qs], b.rtok] + bar23
                            if g == 0 and hsel == 0 and k == 0:
                                w = w + kr
                            qk_tok[hsel][k] = S.emit(
                                "tensor",
                                lambda e, b=b, h=h, k=k, off=off: e.matmul(
                                    b.ap[:, off:SW],
                                    K_sb[:, h * N + k * 128: h * N + k * 128 + 128],
                                    Qs[qs % 2][:, h * SW + off:(h + 1) * SW],
                                    start=True, stop=True),
                                w, inc="CP")
                            b.wtok = qk_tok[hsel][k]

                        emit_qk(0, 0)
                        emit_qk(1, 0)
                        if nk > 1:
                            emit_qk(0, 1)
                            emit_qk(1, 1)
                        emit_finalize()
                        sv = [None, None]
                        dn = [None, None]
                        for k in range(nk):
                            kp = k % 2
                            o = k - 4 * qs
                            off = max(o, 0) * 128
                            for hsel in range(2):
                                bs = ps_s[hsel][kp]
                                t_sq1 = S.emit(
                                    "scalar",
                                    lambda e, bs=bs, hsel=hsel, kp=kp, off=off:
                                        e.activation(
                                            out=T1[hsel][kp][:, off:SW],
                                            in_=bs.ap[:, off:SW],
                                            func=AF.Square),
                                    [qk_tok[hsel][k], t1_cons[hsel][kp]] + bar23,
                                    inc="CA")
                                bs.rtok = t_sq1
                                dep = t_sq1
                                if o >= 0:
                                    dep = S.emit(
                                        "vector",
                                        lambda e, hsel=hsel, kp=kp, off=off:
                                            e.tensor_mul(
                                                T1[hsel][kp][:, off:off + 128],
                                                T1[hsel][kp][:, off:off + 128],
                                                masks[:]),
                                        [t_sq1, t_mk] + bar23, inc="CD")
                                sq_eng, sq_sem = (("gpsimd", "CG") if hsel
                                                  else ("vector", "CD"))
                                sq_tok[hsel][k] = S.emit(
                                    sq_eng,
                                    lambda e, hsel=hsel, kp=kp, off=off:
                                        e.tensor_mul(
                                            S4[hsel][kp][:, off:SW],
                                            T1[hsel][kp][:, off:SW],
                                            T1[hsel][kp][:, off:SW]),
                                    [dep, s4_cons[hsel][kp]] + bar23, inc=sq_sem)
                                t1_cons[hsel][kp] = sq_tok[hsel][k]
                            if k + 2 < nk:
                                emit_qk(0, k + 2)
                                emit_qk(1, k + 2)
                            for hsel in range(2):
                                bo, bd = ps_o[hsel], ps_d[hsel]
                                h = hh[hsel]
                                sv[hsel] = S.emit(
                                    "tensor",
                                    lambda e, bo=bo, hsel=hsel, kp=kp, h=h,
                                           k=k, off=off,
                                           s0=(k == 0), s1=(k == nk - 1):
                                        e.matmul(
                                            bo.ap[:, off:SW],
                                            V_sb[:, k * D + h * 128:
                                                 k * D + (h + 1) * 128],
                                            S4[hsel][kp][:, off:SW],
                                            start=s0, stop=s1),
                                    [sq_tok[hsel][k]] + vt_full +
                                    ([bo.rtok] if k == 0 else []), inc="CP")
                                dn[hsel] = S.emit(
                                    "tensor",
                                    lambda e, bd=bd, hsel=hsel, kp=kp, off=off,
                                           s0=(k == 0), s1=(k == nk - 1):
                                        e.matmul(
                                            bd.ap[:, off:SW], onesM[:],
                                            S4[hsel][kp][:, off:SW],
                                            start=s0, stop=s1),
                                    [t_ones] + ([bd.rtok] if k == 0 else []),
                                    inc="CP")
                                s4_cons[hsel][kp] = dn[hsel]
                        if g == H // 2 - 1:
                            qs_cons[qs % 2] = dn[1]
                        pend.append((hh[0], qs, 0, sv[0], dn[0]))
                        pend.append((hh[1], qs, 1, sv[1], dn[1]))
                    if qs + 1 < NSTRIP:
                        emit_finalize()
                        for kk in range(4 * qs + 4, 4 * qs + 8):
                            load_vtile(kk)
                emit_finalize()
                o_store_done = [("STA", S.cnt["STA"]), ("STB", S.cnt["STB"])]

        with contextlib.ExitStack() as c4:
            E4 = c4.enter_context
            bar34 = S.bar()
            O_sb = E4(nc.sbuf_tensor("O_sb", [128, H * N], BF16))
            sstage = [E4(nc.sbuf_tensor(f"sstage{i}", [128, N], F32))
                      for i in range(2)]
            grp4 = [[Bank(E4(nc.psum_tensor(f"p4{g}_{s}", [128, SW], F32)))
                     for s in range(4)] for g in range(2)]
            lt = None
            for j in range(CH):
                lt = S.dma("sync", "LDX", O_sb[:, j * N:(j + 1) * N],
                           o_sc[j, :, :],
                           waits=(o_store_done + bar34) if j == 0 else [])
            o_ld = lt
            sst_cons = [None, None]
            for j in range(CH):
                par, wtok = wload(wo[j], (0, D), CH)
                banks = grp4[j % 2]
                mmtok = None
                for i in range(CH):
                    for st in range(4):
                        b = banks[st]
                        w = ([wtok, o_ld] + bar34) if (i == 0 and st == 0) else []
                        if i == 0:
                            w = w + [b.rtok]
                        mmtok = S.emit(
                            "tensor",
                            lambda e, b=b, par=par, i=i, st=st,
                                   s0=(i == 0), s1=(i == CH - 1):
                                e.matmul(b.ap[:],
                                         wslot[par][:, i * 128:(i + 1) * 128],
                                         O_sb[:, i * N + st * SW:
                                              i * N + (st + 1) * SW],
                                         start=s0, stop=s1),
                            w, inc="CP" if i == CH - 1 else None)
                        if i == CH - 1:
                            b.wtok = mmtok
                wdone(par, mmtok)
                last = None
                for st in range(4):
                    b = banks[st]
                    last = S.emit(
                        "scalar",
                        lambda e, b=b, j=j, st=st: e.activation(
                            out=sstage[j % 2][:, st * SW:(st + 1) * SW],
                            in_=b.ap[:], func=AF.Identity),
                        [b.wtok, sst_cons[j % 2]] + bar34, inc="CA")
                    b.rtok = last
                sst_cons[j % 2] = S.dma("sync", ST[j % 2], s_sc[j, :, :],
                                        sstage[j % 2][:], waits=[last])
            s_store_done = [("STA", S.cnt["STA"]), ("STB", S.cnt["STB"])]

        with contextlib.ExitStack() as c5:
            E5 = c5.enter_context
            bar45 = S.bar()
            xn2 = E5(nc.sbuf_tensor("xn2", [128, CH * N], BF16))
            h_sb = E5(nc.sbuf_tensor("h_sb", [128, JI * 1024], BF16))
            sg = [E5(nc.sbuf_tensor(f"sg{i}", [128, 1024], F32)) for i in range(2)]
            gg = [E5(nc.sbuf_tensor(f"gg{i}", [128, SW], F32)) for i in range(2)]
            sx = [E5(nc.sbuf_tensor(f"sx{i}", [128, 1024], F32)) for i in range(2)]
            outc = [E5(nc.sbuf_tensor(f"outc{i}", [128, SW], F32))
                    for i in range(2)]
            scl = E5(nc.sbuf_tensor("scl", [128, 2 * JI + CH], F32))
            gs_sb = scl[:, 0:JI]
            us_sb = scl[:, JI:2 * JI]
            ds_sb = scl[:, 2 * JI:2 * JI + CH]
            t_s1 = S.dma("sync", "LDX", gs_sb, gs[:, :], waits=bar45 + s_store_done)
            t_s2 = S.dma("sync", "LDX", us_sb, us[:, :])
            t_s3 = S.dma("sync", "LDX", ds_sb, ds[:, :])
            pg = [[Bank(E5(nc.psum_tensor(f"pg{g}_{s}", [128, SW], F32)))
                   for s in range(2)] for g in range(2)]
            pu = [[Bank(E5(nc.psum_tensor(f"pu{g}_{s}", [128, SW], F32)))
                   for s in range(2)] for g in range(2)]
            lt = None
            for i in range(CH):
                lt = S.dma("sync", "LDX", xn2[:, i * N:(i + 1) * N],
                           xn_sc[i, :, :],
                           waits=(xn_spill + bar45) if i == 0 else [])
            xn_ld = lt
            sg_cons = [None, None]
            gg_cons = [None, None]
            sx_cons = [None, None]
            out_st = [None, None]
            fi = [0]
            sx_pref = [None]

            def emit_sx_load(n, j, st_idx, first):
                lpar = n % 2
                w = [sx_cons[lpar]]
                if first:
                    w = w + s_store_done + bar45
                S.dma("sync", LD[lpar], sx[lpar][:, 0:SW],
                      s_sc[j, :, st_idx * SW:(st_idx + 1) * SW], waits=w)
                sx_pref[0] = S.dma("sync", LD[lpar], sx[lpar][:, SW:2 * SW],
                                   xT[j * 128:(j + 1) * 128,
                                      st_idx * SW:(st_idx + 1) * SW], waits=[])
                return sx_pref[0]

            ld_tok_cur = [None]
            for half in range(2):
                hbase = half * 1024
                h_done = None
                for j in range(JI):
                    par, wt1 = wload(gw[j], (0, D), CH)
                    wtok = S.dma("gpsimd", WS[par],
                                 wslot[par][:, D:2 * D].rearrange(
                                     "p (i n) -> p i n", i=CH),
                                 uw[j], waits=[])
                    gb = pg[j % 2]
                    ub = pu[j % 2]
                    mmt = None
                    for i in range(CH):
                        for s2 in range(2):
                            b = gb[s2]
                            w = ([wtok, xn_ld] + bar45) if (i == 0 and s2 == 0) else []
                            if i == 0:
                                w = w + [b.rtok]
                            t = S.emit(
                                "tensor",
                                lambda e, b=b, par=par, i=i, s2=s2, hb=hbase,
                                       s0=(i == 0), s1=(i == CH - 1):
                                    e.matmul(
                                        b.ap[:],
                                        wslot[par][:, i * 128:(i + 1) * 128],
                                        xn2[:, i * N + hb + s2 * SW:
                                            i * N + hb + s2 * SW + SW],
                                        start=s0, stop=s1),
                                w, inc="CP" if i == CH - 1 else None)
                            if i == CH - 1:
                                b.wtok = t
                    for i in range(CH):
                        for s2 in range(2):
                            b = ub[s2]
                            w = [b.rtok] if i == 0 else []
                            mmt = S.emit(
                                "tensor",
                                lambda e, b=b, par=par, i=i, s2=s2, hb=hbase,
                                       s0=(i == 0), s1=(i == CH - 1):
                                    e.matmul(
                                        b.ap[:],
                                        wslot[par][:, D + i * 128:
                                                   D + (i + 1) * 128],
                                        xn2[:, i * N + hb + s2 * SW:
                                            i * N + hb + s2 * SW + SW],
                                        start=s0, stop=s1),
                                w, inc="CP" if i == CH - 1 else None)
                            if i == CH - 1:
                                b.wtok = mmt
                    wdone(par, mmt)
                    for s2 in range(2):
                        b = gb[s2]
                        t_sg = S.emit(
                            "scalar",
                            lambda e, b=b, j=j, s2=s2, par2=(j % 2): e.activation(
                                out=sg[par2][:, s2 * SW:(s2 + 1) * SW],
                                in_=b.ap[:], func=AF.Silu,
                                scale=gs_sb[:, j:j + 1]),
                            [b.wtok, xn_ld, sg_cons[j % 2]] + bar45, inc="CA")
                        b.rtok = t_sg
                        bu = ub[s2]
                        t_h = S.emit(
                            "vector",
                            lambda e, bu=bu, j=j, s2=s2, par2=(j % 2):
                                e.scalar_tensor_tensor(
                                    h_sb[:, j * 1024 + s2 * SW:
                                         j * 1024 + s2 * SW + SW],
                                    bu.ap[:], us_sb[:, j:j + 1],
                                    sg[par2][:, s2 * SW:(s2 + 1) * SW],
                                    ALU.mult, ALU.mult),
                            [bu.wtok, t_sg, xn_ld] + bar45, inc="CD")
                        bu.rtok = t_h
                        h_done = t_h
                    sg_cons[j % 2] = h_done
                for j in range(CH):
                    par, wtok = wload(dw[j], (0, NI), JI)
                    db = pg[j % 2]
                    mmt = None
                    for i in range(JI):
                        for s2 in range(2):
                            b = db[s2]
                            w = [wtok, h_done] if (i == 0 and s2 == 0) else []
                            if i == 0:
                                w = w + [b.rtok]
                            mmt = S.emit(
                                "tensor",
                                lambda e, b=b, par=par, i=i, s2=s2,
                                       s0=(i == 0), s1=(i == JI - 1):
                                    e.matmul(
                                        b.ap[:],
                                        wslot[par][:, i * 128:(i + 1) * 128],
                                        h_sb[:, i * 1024 + s2 * SW:
                                             i * 1024 + s2 * SW + SW],
                                        start=s0, stop=s1),
                                w, inc="CP" if i == JI - 1 else None)
                            if i == JI - 1:
                                b.wtok = mmt
                    wdone(par, mmt)
                    for s2 in range(2):
                        st_idx = 2 * half + s2
                        n = fi[0]
                        lpar = n % 2
                        if n == 0:
                            ld_tok_cur[0] = emit_sx_load(0, j, st_idx, True)
                        ld_tok = ld_tok_cur[0]
                        if not (half == 1 and j == CH - 1 and s2 == 1):
                            jn, s2n, halfn = j, s2 + 1, half
                            if s2n == 2:
                                s2n = 0
                                jn += 1
                            if jn == CH:
                                jn = 0
                                halfn += 1
                            ld_tok_cur[0] = emit_sx_load(
                                n + 1, jn, 2 * halfn + s2n, False)
                        b = db[s2]
                        t_gg = S.emit(
                            "scalar",
                            lambda e, b=b, j=j, par2=(j % 2): e.activation(
                                out=gg[par2][:], in_=b.ap[:], func=AF.Silu,
                                scale=ds_sb[:, j:j + 1]),
                            [b.wtok, xn_ld, gg_cons[j % 2]], inc="CA")
                        b.rtok = t_gg
                        t_f1 = S.emit(
                            "vector",
                            lambda e, par2=(j % 2), lpar=lpar: e.tensor_mul(
                                outc[lpar][:], gg[par2][:], sx[lpar][:, 0:SW]),
                            [t_gg, ld_tok, out_st[lpar]], inc="CD")
                        gg_cons[j % 2] = t_f1
                        t_f2 = S.emit(
                            "vector",
                            lambda e, lpar=lpar: e.tensor_add(
                                outc[lpar][:], outc[lpar][:],
                                sx[lpar][:, SW:2 * SW]),
                            [], inc="CD")
                        sx_cons[lpar] = t_f2
                        out_st[lpar] = S.dma(
                            "sync", ST[lpar],
                            outT[j * 128:(j + 1) * 128,
                                 st_idx * SW:(st_idx + 1) * SW],
                            outc[lpar][:], waits=[t_f2])
                        fi[0] += 1

        S.emit("sync", lambda e: e.nop(),
               [("STA", S.cnt["STA"]), ("STB", S.cnt["STB"])])

        sems = {name: E(nc.semaphore(name)) for name in S.names}
        block = E(nc.Block())

        def mk(ename):
            def run(eng):
                for w, fn, inc, n in S.q[ename]:
                    for s, v in w[:-1]:
                        eng.wait_ge(sems[s], v)
                    ins = fn(eng)
                    if w:
                        s, v = w[-1]
                        ins._wait_ge(sems[s], v)
                    if inc is not None:
                        ins.then_inc(sems[inc], n)
            return run

        block.sync(mk("sync"))
        block.gpsimd(mk("gpsimd"))
        block.scalar(mk("scalar"))
        block.vector(mk("vector"))
        block.tensor(mk("tensor"))
    return nc


def _tile4(w, jn, inn):
    j, n = jn
    i, p = inn
    return np.ascontiguousarray(
        np.asarray(w, dtype=np.float32).reshape(j, n, i, p).transpose(0, 3, 2, 1))


def _consts():
    half = 64
    try:
        import jax
        import jax.numpy as jnp
        with jax.default_device(jax.devices("cpu")[0]):
            tj = 10000.0 ** (-jnp.arange(half, dtype=jnp.float32) * 2.0 / 128)
            aj = (jnp.arange(N, dtype=jnp.float32)[:, None] * tj[None, :])
            cosv = np.asarray(jnp.cos(aj)).T
            sinv = np.asarray(jnp.sin(aj)).T
        cossin = np.concatenate([cosv, sinv], axis=0).astype(np.float32)
    except Exception:
        theta = (10000.0 ** (-np.arange(half, dtype=np.float64) * 2.0 / 128.0))
        ang = theta[:, None] * np.arange(N, dtype=np.float64)[None, :]
        cossin = np.concatenate([np.cos(ang), np.sin(ang)],
                                axis=0).astype(np.float32)
    maskv = (np.arange(128)[:, None] <= np.arange(128)[None, :]).astype(np.float32)
    return cossin, np.ascontiguousarray(maskv)


def kernel(L0_attn_x, L0_attn_Wq, L0_attn_Wk, L0_attn_Wv, L0_attn_Wo,
           L0_ffn_gate_w, L0_ffn_up_w, L0_ffn_down_w, L0_ffn_gate_s,
           L0_ffn_up_s, L0_ffn_down_s):
    x = np.asarray(L0_attn_x, dtype=np.float32)
    B = x.shape[0]
    cossin, maskv = _consts()
    shared = {
        "wq_hl": _tile4(L0_attn_Wq, (CH, 128), (CH, 128)),
        "wk_hl": _tile4(L0_attn_Wk, (CH, 128), (CH, 128)),
        "wv_hl": _tile4(L0_attn_Wv, (CH, 128), (CH, 128)),
        "wo_hl": _tile4(L0_attn_Wo, (CH, 128), (CH, 128)),
        "gate_hl": _tile4(L0_ffn_gate_w, (JI, 128), (CH, 128)),
        "up_hl": _tile4(L0_ffn_up_w, (JI, 128), (CH, 128)),
        "down_hl": _tile4(L0_ffn_down_w, (CH, 128), (JI, 128)),
        "gate_s_t": np.ascontiguousarray(
            np.asarray(L0_ffn_gate_s, np.float32).reshape(JI, 128).T),
        "up_s_t": np.ascontiguousarray(
            np.asarray(L0_ffn_up_s, np.float32).reshape(JI, 128).T),
        "down_s_t": np.ascontiguousarray(
            np.asarray(L0_ffn_down_s, np.float32).reshape(CH, 128).T),
        "cossin": cossin,
        "maskv": maskv,
    }
    in_maps = []
    for c in range(B):
        m = dict(shared)
        m["xT"] = np.ascontiguousarray(x[c].T)
        in_maps.append(m)
    nc = build_nc(debug=False)
    res = run_bass_kernel_spmd(nc, in_maps, core_ids=list(range(B)))
    out = np.stack([np.asarray(res.results[c]["outT"]).T for c in range(B)],
                   axis=0)
    return np.ascontiguousarray(out.astype(np.float32))


# version 18
# speedup vs baseline: 1.0473x; 1.0473x over previous
import contextlib
import numpy as np
import ml_dtypes
import concourse.bass as bass
import concourse.mybir as mybir
from concourse.bass_utils import run_bass_kernel_spmd

F32 = mybir.dt.float32
BF16 = mybir.dt.bfloat16
AF = mybir.ActivationFunctionType
ALU = mybir.AluOpType

D = 2048
N = 2048
CH = D // 128
H = 16
NI = 5632
JI = NI // 128
NSTRIP = 4
SW = 512
KT = N // 128
EPS_LN = 1e-5
EPS_ATT = 1e-6

ENGS = ("sync", "gpsimd", "scalar", "vector", "tensor")


class Sched:

    def __init__(self):
        self.q = {e: [] for e in ENGS}
        self.cnt = {}
        self.waited = {e: {} for e in ENGS}
        self.names = []
        self.last_comp = {e: None for e in ENGS}

    def sem(self, name):
        if name not in self.cnt:
            self.cnt[name] = 0
            self.names.append(name)
        return name

    def emit(self, eng, fn, waits=(), inc=None, n=1):
        waits = list(waits)
        if eng != "tensor" and (inc in ("CP", "CA", "CD", "CG") or inc is None):
            waits.append(self.last_comp[eng])
        w = []
        for t in waits:
            if t is None:
                continue
            s, v = t
            assert v <= self.cnt[s], ("wait-on-future", s, v, self.cnt[s])
            if v > self.waited[eng].get(s, 0):
                self.waited[eng][s] = v
                w.append((s, v))
        tok = None
        if inc is not None:
            self.cnt[inc] += n
            tok = (inc, self.cnt[inc])
            if inc in ("CP", "CA", "CD", "CG"):
                self.last_comp[eng] = tok
        self.q[eng].append((w, fn, inc, n))
        return tok

    def dma(self, eng, sem, out, in_, waits=(), transpose=False):
        if transpose:
            fn = lambda e, o=out, i=in_: e.dma_start(out=o, in_=i, transpose=True)
        else:
            fn = lambda e, o=out, i=in_: e.dma_start(out=o, in_=i)
        return self.emit(eng, fn, waits, inc=sem, n=16)

    def bar(self):
        return [("CP", self.cnt["CP"]), ("CA", self.cnt["CA"]),
                ("CD", self.cnt["CD"]), ("CG", self.cnt["CG"])]


class Bank:
    __slots__ = ("ap", "wtok", "rtok")

    def __init__(self, ap):
        self.ap = ap
        self.wtok = None
        self.rtok = None


def build_nc(debug=False):
    nc = bass.Bass("TRN2", target_bir_lowering=False)
    S = Sched()
    for s in ("LDA", "LDB", "LDX", "STA", "STB", "W0", "W1", "VT0", "VT1",
              "CP", "CA", "CD", "CG"):
        S.sem(s)
    LD = ("LDA", "LDB")
    ST = ("STA", "STB")
    WS = ("W0", "W1")
    VT = ("VT0", "VT1")

    xT = nc.dram_tensor("xT", [D, N], F32, kind="ExternalInput").ap()
    wq = nc.dram_tensor("wq_hl", [CH, 128, CH, 128], F32, kind="ExternalInput").ap()
    wk = nc.dram_tensor("wk_hl", [CH, 128, CH, 128], F32, kind="ExternalInput").ap()
    wv = nc.dram_tensor("wv_hl", [CH, 128, CH, 128], F32, kind="ExternalInput").ap()
    wo = nc.dram_tensor("wo_hl", [CH, 128, CH, 128], F32, kind="ExternalInput").ap()
    gw = nc.dram_tensor("gate_hl", [JI, 128, CH, 128], F32, kind="ExternalInput").ap()
    uw = nc.dram_tensor("up_hl", [JI, 128, CH, 128], F32, kind="ExternalInput").ap()
    dw = nc.dram_tensor("down_hl", [CH, 128, JI, 128], F32, kind="ExternalInput").ap()
    gs = nc.dram_tensor("gate_s_t", [128, JI], F32, kind="ExternalInput").ap()
    us = nc.dram_tensor("up_s_t", [128, JI], F32, kind="ExternalInput").ap()
    ds = nc.dram_tensor("down_s_t", [128, CH], F32, kind="ExternalInput").ap()
    cs = nc.dram_tensor("cossin", [128, N], F32, kind="ExternalInput").ap()
    mk = nc.dram_tensor("maskv", [128, 128], F32, kind="ExternalInput").ap()
    outT = nc.dram_tensor("outT", [D, N], F32, kind="ExternalOutput").ap()

    ksc = "ExternalOutput" if debug else "Internal"
    xn_sc = nc.dram_tensor("xn_sc", [CH, 128, N], BF16, kind=ksc).ap()
    q_sc = nc.dram_tensor("q_sc", [CH, 128, N], BF16, kind=ksc).ap()
    v_sc = nc.dram_tensor("v_sc", [D, N], BF16, kind=ksc).ap()
    o_sc = nc.dram_tensor("o_sc", [CH, 128, N], BF16, kind=ksc).ap()
    s_sc = nc.dram_tensor("s_sc", [CH, 128, N], F32, kind=ksc).ap()

    xT_r = xT.rearrange("(i p) t -> i p t", p=128)
    xn_sc_r = xn_sc.rearrange("i p t -> p i t")

    with contextlib.ExitStack() as ctx:
        E = ctx.enter_context
        wslot = [E(nc.sbuf_tensor(f"wslot{i}", [128, NI], BF16)) for i in range(2)]
        ones_f = E(nc.sbuf_tensor("ones_f", [128, 1], F32))
        ones_b = E(nc.sbuf_tensor("ones_b", [128, 1], BF16))
        onesM = E(nc.sbuf_tensor("onesM", [128, 128], BF16))
        negone = E(nc.sbuf_tensor("negone", [128, 1], F32))
        t_ones = S.emit("vector", lambda e: (e.memset(ones_f[:], 1.0),
                                             e.memset(ones_b[:], 1.0),
                                             e.memset(onesM[:], 1.0),
                                             e.memset(negone[:], -1.0))[-1],
                        inc="CD")
        wcons = [None, None]
        wc = [0]

        def wload(dram_ap, cols, ncols_i):
            par = wc[0] % 2
            view = wslot[par][:, cols[0]:cols[1]].rearrange(
                "p (i n) -> p i n", i=ncols_i)
            tok = S.dma("gpsimd", WS[par], view, dram_ap, waits=[wcons[par]])
            return par, tok

        def wdone(par, tok):
            wcons[par] = tok
            wc[0] += 1

        xn_ready = []
        xn_spill = []
        k_ready = []

        with contextlib.ExitStack() as c13:
            E13 = c13.enter_context
            K_sb = E13(nc.sbuf_tensor("K_sb", [128, H * N], BF16))
            with contextlib.ExitStack() as c12:
                E12 = c12.enter_context
                cossin = E12(nc.sbuf_tensor("cossin_sb", [128, N], F32))
                ones_r = E12(nc.sbuf_tensor("ones_r", [1, 128], F32))
                eps_sb = E12(nc.sbuf_tensor("eps_sb", [1, 1], F32))
                t_ones2 = S.emit("vector",
                                 lambda e: (e.memset(ones_r[:], 1.0),
                                            e.memset(eps_sb[:], EPS_LN))[-1],
                                 inc="CD")
                t_cs = S.dma("sync", "LDX", cossin[:], cs[:, :])
                xn_sb = E12(nc.sbuf_tensor("xn_sb", [128, CH * N], BF16))

                with contextlib.ExitStack() as c1:
                    E1 = c1.enter_context
                    xa = [E1(nc.sbuf_tensor(f"xa{i}", [128, 4 * SW], F32))
                          for i in range(2)]
                    xb = [E1(nc.sbuf_tensor(f"xb{i}", [128, 4 * SW], F32))
                          for i in range(2)]
                    sq = [E1(nc.sbuf_tensor(f"sq{i}", [128, SW], F32))
                          for i in range(2)]
                    tb = [E1(nc.sbuf_tensor(f"tb{i}", [128, SW], F32))
                          for i in range(2)]
                    stat = [E1(nc.sbuf_tensor("stat0", [1, 4 * SW], F32))] * 2
                    ps_sum = [Bank(E1(nc.psum_tensor(f"ps_sum{i}", [1, SW], F32)))
                              for i in range(2)]
                    ps_sq = [Bank(E1(nc.psum_tensor(f"ps_sq{i}", [1, SW], F32)))
                             for i in range(2)]
                    ps_mb = [Bank(E1(nc.psum_tensor(f"ps_mb{i}", [128, SW], F32)))
                             for i in range(2)]
                    ps_rb = [Bank(E1(nc.psum_tensor(f"ps_rb{i}", [128, SW], F32)))
                             for i in range(2)]
                    xT_g = xT.rearrange("(g c p) t -> g p c t", g=4, p=128)

                    xa_cons = [None, None]
                    xa_cons_a = [None, None]
                    xb_cons = [None, None]
                    sq_cons = [None, None]
                    tb_cons = [None, None]
                    rb_tok = [None, None]
                    pe_stats_tok = [None, None]

                    def p1_pass1(s):
                        sp = s % 2
                        pss, psq = ps_sum[sp], ps_sq[sp]
                        for g in range(4):
                            p = g % 2
                            lt = S.dma(
                                "sync", LD[p],
                                xa[p][:].rearrange("q (c t) -> q c t", c=4),
                                xT_g[g][:, :, s * SW:(s + 1) * SW],
                                waits=[xa_cons[p], xa_cons_a[p]])
                            for c in range(4):
                                i = 4 * g + c
                                pq = i % 2
                                xv = xa[p][:, c * SW:(c + 1) * SW]
                                tsq = S.emit(
                                    "scalar",
                                    lambda e, pq=pq, xv=xv: e.activation(
                                        out=sq[pq][:], in_=xv, func=AF.Square),
                                    [lt, sq_cons[pq]], inc="CA")
                                last = S.emit(
                                    "tensor",
                                    lambda e, xv=xv, pss=pss,
                                           s0=(i == 0), s1=(i == CH - 1):
                                        e.matmul(pss.ap[:], ones_f[:], xv,
                                                 start=s0, stop=s1),
                                    [lt, t_ones] +
                                    ([pss.rtok, psq.rtok] if i == 0 else []),
                                    inc="CP")
                                if c == 3:
                                    xa_cons[p] = last
                                    xa_cons_a[p] = tsq
                                sq_cons[pq] = S.emit(
                                    "tensor",
                                    lambda e, pq=pq, psq=psq,
                                           s0=(i == 0), s1=(i == CH - 1):
                                        e.matmul(psq.ap[:], ones_f[:], sq[pq][:],
                                                 start=s0, stop=s1),
                                    [tsq], inc="CP")
                        pe_stats_tok[sp] = sq_cons[(CH - 1) % 2]

                    def p1_stats(s):
                        sp = s % 2
                        pss, psq = ps_sum[sp], ps_sq[sp]
                        mean = stat[sp][0:1, 0:SW]
                        ex2 = stat[sp][0:1, SW:2 * SW]
                        m2 = stat[sp][0:1, 2 * SW:3 * SW]
                        rstd = stat[sp][0:1, 3 * SW:4 * SW]
                        t_mean = S.emit(
                            "vector",
                            lambda e, mean=mean, pss=pss: e.tensor_scalar_mul(
                                mean, pss.ap[:], 1.0 / D),
                            [pe_stats_tok[sp]], inc="CD")
                        pss.rtok = t_mean
                        t_ex2 = S.emit(
                            "vector",
                            lambda e, ex2=ex2, psq=psq: e.tensor_scalar_mul(
                                ex2, psq.ap[:], 1.0 / D),
                            [], inc="CD")
                        psq.rtok = t_ex2
                        S.emit("vector",
                               lambda e, m2=m2, mean=mean: e.tensor_mul(
                                   m2, mean, mean), [], inc="CD")
                        t_var = S.emit("vector",
                                       lambda e, ex2=ex2, m2=m2: e.tensor_sub(
                                           ex2, ex2, m2), [], inc="CD")
                        t_sd = S.emit(
                            "scalar",
                            lambda e, m2=m2, ex2=ex2: e.activation(
                                out=m2, in_=ex2, func=AF.Sqrt,
                                bias=eps_sb[0:1, 0:1]),
                            [t_var, t_ones2], inc="CA")
                        t_rstd = S.emit("vector",
                                        lambda e, rstd=rstd, m2=m2:
                                            e.reciprocal(rstd, m2),
                                        [t_sd], inc="CD")
                        t_mb = S.emit(
                            "tensor",
                            lambda e, sp=sp, mean=mean: e.matmul(
                                ps_mb[sp].ap[:], ones_r[:], mean,
                                start=True, stop=True),
                            [t_mean, t_ones2, ps_mb[sp].rtok], inc="CP")
                        t_rb = S.emit(
                            "tensor",
                            lambda e, sp=sp, rstd=rstd: e.matmul(
                                ps_rb[sp].ap[:], ones_r[:], rstd,
                                start=True, stop=True),
                            [t_rstd, ps_rb[sp].rtok], inc="CP")
                        rb_tok[sp] = (t_mb, t_rb)

                    def p1_pass2(s):
                        sp = s % 2
                        t_mb, t_rb = rb_tok[sp]
                        t_norm = None
                        for g in range(4):
                            p = g % 2
                            lt = S.dma(
                                "sync", VT[p],
                                xb[p][:].rearrange("q (c t) -> q c t", c=4),
                                xT_g[g][:, :, s * SW:(s + 1) * SW],
                                waits=[xb_cons[p]])
                            for c in range(4):
                                i = 4 * g + c
                                pq = i % 2
                                xv = xb[p][:, c * SW:(c + 1) * SW]
                                S.emit(
                                    "vector",
                                    lambda e, pq=pq, xv=xv, sp=sp: e.tensor_sub(
                                        tb[pq][:], xv, ps_mb[sp].ap[:]),
                                    [lt, t_mb, tb_cons[pq]], inc="CD")
                                t_norm = S.emit(
                                    "vector",
                                    lambda e, pq=pq, i=i, s=s, sp=sp: e.tensor_mul(
                                        xn_sb[:, i * N + s * SW:
                                              i * N + (s + 1) * SW],
                                        tb[pq][:], ps_rb[sp].ap[:]),
                                    [t_rb], inc="CD")
                                tb_cons[pq] = t_norm
                                if c == 3:
                                    xb_cons[p] = t_norm
                        ps_mb[sp].rtok = t_norm
                        ps_rb[sp].rtok = t_norm
                        xn_ready.append(t_norm)
                        xn_view = xn_sb[:].rearrange("p (i t) -> p i t", i=CH)
                        xn_spill.append(S.dma(
                            "sync", ST[s % 2],
                            xn_sc_r[:, :, s * SW:(s + 1) * SW],
                            xn_view[:, :, s * SW:(s + 1) * SW],
                            waits=[t_norm]))

                    for s in range(NSTRIP + 1):
                        if s < NSTRIP:
                            p1_pass1(s)
                            p1_stats(s)
                        if s >= 1:
                            p1_pass2(s - 1)

                with contextlib.ExitStack() as c2:
                    E2 = c2.enter_context
                    bar12 = S.bar()
                    qstage = [E2(nc.sbuf_tensor(f"qstage{i}", [128, N], BF16))
                              for i in range(2)]
                    rA1 = [E2(nc.sbuf_tensor(f"rA1_{i}", [64, SW], F32))
                           for i in range(2)]
                    rA2 = [E2(nc.sbuf_tensor(f"rA2_{i}", [64, SW], F32))
                           for i in range(2)]
                    rB1 = [E2(nc.sbuf_tensor(f"rB1_{i}", [64, SW], F32))
                           for i in range(2)]
                    rB2 = [E2(nc.sbuf_tensor(f"rB2_{i}", [64, SW], F32))
                           for i in range(2)]
                    pool_cons = [None, None]
                    grp = [[Bank(E2(nc.psum_tensor(f"pj{g}_{s}", [128, SW], F32)))
                            for s in range(4)] for g in range(2)]
                    stage_cons = [None, None]

                    def proj(wap, dest):
                        last_evac = None
                        for j in range(CH):
                            par, wtok = wload(wap[j], (0, D), CH)
                            banks = grp[j % 2]
                            mmtok = None
                            for i in range(CH):
                                for st in range(4):
                                    b = banks[st]
                                    w = ([wtok] + xn_ready + bar12
                                         if (i == 0 and st == 0) else [])
                                    if i == 0:
                                        w = w + [b.rtok]
                                    mmtok = S.emit(
                                        "tensor",
                                        lambda e, b=b, par=par, i=i, st=st,
                                               s0=(i == 0), s1=(i == CH - 1):
                                            e.matmul(
                                                b.ap[:],
                                                wslot[par][:, i * 128:(i + 1) * 128],
                                                xn_sb[:, i * N + st * SW:
                                                      i * N + (st + 1) * SW],
                                                start=s0, stop=s1),
                                        w, inc="CP" if i == CH - 1 else None)
                                    if i == CH - 1:
                                        b.wtok = mmtok
                            wdone(par, mmtok)
                            for st in range(4):
                                b = banks[st]
                                if dest == "v":
                                    tok = S.emit(
                                        "scalar",
                                        lambda e, b=b, j=j, st=st: e.activation(
                                            out=qstage[j % 2][:,
                                                              st * SW:(st + 1) * SW],
                                            in_=b.ap[:], func=AF.Identity),
                                        [b.wtok, stage_cons[j % 2]] + bar12,
                                        inc="CA")
                                    b.rtok = tok
                                    last_evac = tok
                                else:
                                    co = cossin[0:64, st * SW:(st + 1) * SW]
                                    si = cossin[64:128, st * SW:(st + 1) * SW]
                                    if dest == "k":
                                        d0 = K_sb[0:64,
                                                  j * N + st * SW: j * N + (st + 1) * SW]
                                        d1 = K_sb[64:128,
                                                  j * N + st * SW: j * N + (st + 1) * SW]
                                        war = []
                                    else:
                                        d0 = qstage[j % 2][0:64, st * SW:(st + 1) * SW]
                                        d1 = qstage[j % 2][64:128, st * SW:(st + 1) * SW]
                                        war = [stage_cons[j % 2]]
                                    rp = (j * 4 + st) % 2
                                    S.emit(
                                        "vector",
                                        lambda e, b=b, co=co, rp=rp: e.tensor_mul(
                                            rA1[rp][:], b.ap[0:64, :], co),
                                        [b.wtok, t_cs] + war + bar12, inc="CD")
                                    S.emit(
                                        "vector",
                                        lambda e, b=b, si=si, rp=rp: e.tensor_mul(
                                            rA2[rp][:], b.ap[64:128, :], si),
                                        [], inc="CD")
                                    S.emit(
                                        "vector",
                                        lambda e, d0=d0, rp=rp: e.tensor_sub(
                                            d0, rA1[rp][:], rA2[rp][:]),
                                        [], inc="CD")
                                    S.emit(
                                        "vector",
                                        lambda e, b=b, si=si, rp=rp: e.tensor_mul(
                                            rB1[rp][:], b.ap[0:64, :], si),
                                        [], inc="CD")
                                    tok = S.emit(
                                        "vector",
                                        lambda e, b=b, co=co, rp=rp: e.tensor_mul(
                                            rB2[rp][:], b.ap[64:128, :], co),
                                        [], inc="CD")
                                    b.rtok = tok
                                    last_evac = S.emit(
                                        "vector",
                                        lambda e, d1=d1, rp=rp: e.tensor_add(
                                            d1, rB1[rp][:], rB2[rp][:]),
                                        [], inc="CD")
                            if dest == "q":
                                stage_cons[j % 2] = S.dma(
                                    "sync", ST[j % 2], q_sc[j, :, :],
                                    qstage[j % 2][:], waits=[last_evac])
                            elif dest == "v":
                                stage_cons[j % 2] = S.dma(
                                    "sync", ST[j % 2],
                                    v_sc[j * 128:(j + 1) * 128, :],
                                    qstage[j % 2][:], waits=[last_evac])
                            else:
                                k_ready.append(last_evac)
                        return last_evac

                    proj(wq, "q")
                    proj(wk, "k")
                    v_last = proj(wv, "v")
                    qv_store_done = [("STA", S.cnt["STA"]), ("STB", S.cnt["STB"])]

            with contextlib.ExitStack() as c3:
                E3 = c3.enter_context
                bar23 = S.bar()
                masks = E3(nc.sbuf_tensor("masks", [128, 128], F32))
                t_mk = S.dma("sync", "LDX", masks[:], mk[:, :],
                             waits=bar23 + qv_store_done)
                Qs = [E3(nc.sbuf_tensor(f"Qs{i}", [128, H * SW], BF16))
                      for i in range(2)]
                V_sb = E3(nc.sbuf_tensor("V_sb", [128, KT * D], BF16))
                S4 = [[E3(nc.sbuf_tensor(f"S4_{a}_{b}", [128, SW], BF16))
                       for b in range(2)] for a in range(2)]
                T1 = [[E3(nc.sbuf_tensor(f"T1_{a}_{b}", [128, SW], F32))
                       for b in range(2)] for a in range(2)]
                rc1 = [E3(nc.sbuf_tensor(f"rc1_{i}", [128, SW], F32))
                       for i in range(2)]
                Obf = [E3(nc.sbuf_tensor(f"Obf{i}", [128, SW], BF16))
                       for i in range(2)]
                ps_s = [[Bank(E3(nc.psum_tensor(f"ps_s{a}_{b}", [128, SW], F32)))
                         for b in range(2)] for a in range(2)]
                ps_o = [Bank(E3(nc.psum_tensor(f"ps_o{i}", [128, SW], F32)))
                        for i in range(2)]
                ps_d = [Bank(E3(nc.psum_tensor(f"ps_d{i}", [128, SW], F32)))
                        for i in range(2)]

                kr = [k_ready[-1], v_last]
                qs_load_tok = [None] * NSTRIP
                qs_cons = [None, None]
                s4_cons = [[None] * 2 for _ in range(2)]
                t1_cons = [[None] * 2 for _ in range(2)]
                obf_st = [None, None]
                rc1_cons = [None, None]

                def load_qstrip(qs):
                    par = qs % 2
                    lt = None
                    for j in range(CH):
                        lt = S.dma("sync", LD[par],
                                   Qs[par][:, j * SW:(j + 1) * SW],
                                   q_sc[j, :, qs * SW:(qs + 1) * SW],
                                   waits=([qs_cons[par]] + qv_store_done + bar23)
                                   if j == 0 else [])
                    qs_load_tok[qs] = lt

                def load_vtile(k):
                    S.dma("sync", VT[k % 2], V_sb[:, k * D:(k + 1) * D],
                          v_sc[:, k * 128:(k + 1) * 128],
                          waits=qv_store_done + bar23, transpose=True)

                load_qstrip(0)
                for k in range(4):
                    load_vtile(k)

                pend = []

                def emit_finalize():
                    while pend:
                        h, qs, hsel, sv_stop, dn_stop = pend.pop(0)
                        bo, bd = ps_o[hsel], ps_d[hsel]
                        hp = h % 2
                        t_dm = S.emit(
                            "vector",
                            lambda e, bd=bd, hp=hp: e.tensor_scalar_max(
                                rc1[hp][:], bd.ap[:], EPS_ATT),
                            [dn_stop, rc1_cons[hp]], inc="CD")
                        bd.rtok = t_dm
                        t_ln = S.emit(
                            "scalar",
                            lambda e, hp=hp: e.activation(
                                out=rc1[hp][:], in_=rc1[hp][:], func=AF.Ln),
                            [t_dm], inc="CA")
                        t_ex = S.emit(
                            "scalar",
                            lambda e, hp=hp: e.activation(
                                out=rc1[hp][:], in_=rc1[hp][:], func=AF.Exp,
                                scale=negone[:, 0:1]),
                            [t_ln, t_ones], inc="CA")
                        t_ob = S.emit(
                            "vector",
                            lambda e, bo=bo, hp=hp: e.tensor_mul(
                                Obf[hp][:], bo.ap[:], rc1[hp][:]),
                            [t_ex, sv_stop, obf_st[hp]], inc="CD")
                        bo.rtok = t_ob
                        rc1_cons[hp] = t_ob
                        obf_st[hp] = S.dma(
                            "sync", ST[hp], o_sc[h, :, qs * SW:(qs + 1) * SW],
                            Obf[hp][:], waits=[t_ob])

                for qs in range(NSTRIP):
                    nk = 4 * qs + 4
                    vt_full = [("VT0", S.cnt["VT0"]), ("VT1", S.cnt["VT1"])]
                    if qs + 1 < NSTRIP:
                        load_qstrip(qs + 1)
                    for g in range(H // 2):
                        hh = (2 * g, 2 * g + 1)
                        qk_tok = [[None] * nk, [None] * nk]
                        sq_tok = [[None] * nk, [None] * nk]

                        def emit_qk(hsel, k, qs=qs, qk_tok=qk_tok, hh=hh):
                            kp = k % 2
                            b = ps_s[hsel][kp]
                            h = hh[hsel]
                            o = k - 4 * qs
                            off = max(o, 0) * 128
                            w = [qs_load_tok[qs], b.rtok] + bar23
                            if g == 0 and hsel == 0 and k == 0:
                                w = w + kr
                            qk_tok[hsel][k] = S.emit(
                                "tensor",
                                lambda e, b=b, h=h, k=k, off=off: e.matmul(
                                    b.ap[:, off:SW],
                                    K_sb[:, h * N + k * 128: h * N + k * 128 + 128],
                                    Qs[qs % 2][:, h * SW + off:(h + 1) * SW],
                                    start=True, stop=True),
                                w, inc="CP")
                            b.wtok = qk_tok[hsel][k]

                        emit_qk(0, 0)
                        emit_qk(1, 0)
                        if nk > 1:
                            emit_qk(0, 1)
                            emit_qk(1, 1)
                        emit_finalize()
                        sv = [None, None]
                        dn = [None, None]
                        for k in range(nk):
                            kp = k % 2
                            o = k - 4 * qs
                            off = max(o, 0) * 128
                            for hsel in range(2):
                                bs = ps_s[hsel][kp]
                                t_sq1 = S.emit(
                                    "scalar",
                                    lambda e, bs=bs, hsel=hsel, kp=kp, off=off:
                                        e.activation(
                                            out=T1[hsel][kp][:, off:SW],
                                            in_=bs.ap[:, off:SW],
                                            func=AF.Square),
                                    [qk_tok[hsel][k], t1_cons[hsel][kp]] + bar23,
                                    inc="CA")
                                bs.rtok = t_sq1
                                dep = t_sq1
                                if o >= 0:
                                    dep = S.emit(
                                        "vector",
                                        lambda e, hsel=hsel, kp=kp, off=off:
                                            e.tensor_mul(
                                                T1[hsel][kp][:, off:off + 128],
                                                T1[hsel][kp][:, off:off + 128],
                                                masks[:]),
                                        [t_sq1, t_mk] + bar23, inc="CD")
                                sq_eng, sq_sem = (("gpsimd", "CG") if hsel
                                                  else ("vector", "CD"))
                                sq_tok[hsel][k] = S.emit(
                                    sq_eng,
                                    lambda e, hsel=hsel, kp=kp, off=off:
                                        e.tensor_mul(
                                            S4[hsel][kp][:, off:SW],
                                            T1[hsel][kp][:, off:SW],
                                            T1[hsel][kp][:, off:SW]),
                                    [dep, s4_cons[hsel][kp]] + bar23, inc=sq_sem)
                                t1_cons[hsel][kp] = sq_tok[hsel][k]
                            if k + 2 < nk:
                                emit_qk(0, k + 2)
                                emit_qk(1, k + 2)
                            for hsel in range(2):
                                bo = ps_o[hsel]
                                h = hh[hsel]
                                sv[hsel] = S.emit(
                                    "tensor",
                                    lambda e, bo=bo, hsel=hsel, kp=kp, h=h,
                                           k=k, off=off,
                                           s0=(k == 0), s1=(k == nk - 1):
                                        e.matmul(
                                            bo.ap[:, off:SW],
                                            V_sb[:, k * D + h * 128:
                                                 k * D + (h + 1) * 128],
                                            S4[hsel][kp][:, off:SW],
                                            start=s0, stop=s1),
                                    [sq_tok[hsel][k]] + vt_full +
                                    ([bo.rtok] if k == 0 else []), inc="CP")
                            for hsel in range(2):
                                bd = ps_d[hsel]
                                dn[hsel] = S.emit(
                                    "tensor",
                                    lambda e, bd=bd, hsel=hsel, kp=kp, off=off,
                                           s0=(k == 0), s1=(k == nk - 1):
                                        e.matmul(
                                            bd.ap[:, off:SW], onesM[:],
                                            S4[hsel][kp][:, off:SW],
                                            start=s0, stop=s1),
                                    [t_ones] + ([bd.rtok] if k == 0 else []),
                                    inc="CP")
                                s4_cons[hsel][kp] = dn[hsel]
                        if g == H // 2 - 1:
                            qs_cons[qs % 2] = dn[1]
                        pend.append((hh[0], qs, 0, sv[0], dn[0]))
                        pend.append((hh[1], qs, 1, sv[1], dn[1]))
                    if qs + 1 < NSTRIP:
                        emit_finalize()
                        for kk in range(4 * qs + 4, 4 * qs + 8):
                            load_vtile(kk)
                emit_finalize()
                o_store_done = [("STA", S.cnt["STA"]), ("STB", S.cnt["STB"])]

        with contextlib.ExitStack() as c4:
            E4 = c4.enter_context
            bar34 = S.bar()
            O_sb = E4(nc.sbuf_tensor("O_sb", [128, H * N], BF16))
            sstage = [E4(nc.sbuf_tensor(f"sstage{i}", [128, N], F32))
                      for i in range(2)]
            grp4 = [[Bank(E4(nc.psum_tensor(f"p4{g}_{s}", [128, SW], F32)))
                     for s in range(4)] for g in range(2)]
            lt = None
            for j in range(CH):
                lt = S.dma("sync", "LDX", O_sb[:, j * N:(j + 1) * N],
                           o_sc[j, :, :],
                           waits=(o_store_done + bar34) if j == 0 else [])
            o_ld = lt
            sst_cons = [None, None]
            for j in range(CH):
                par, wtok = wload(wo[j], (0, D), CH)
                banks = grp4[j % 2]
                mmtok = None
                for i in range(CH):
                    for st in range(4):
                        b = banks[st]
                        w = ([wtok, o_ld] + bar34) if (i == 0 and st == 0) else []
                        if i == 0:
                            w = w + [b.rtok]
                        mmtok = S.emit(
                            "tensor",
                            lambda e, b=b, par=par, i=i, st=st,
                                   s0=(i == 0), s1=(i == CH - 1):
                                e.matmul(b.ap[:],
                                         wslot[par][:, i * 128:(i + 1) * 128],
                                         O_sb[:, i * N + st * SW:
                                              i * N + (st + 1) * SW],
                                         start=s0, stop=s1),
                            w, inc="CP" if i == CH - 1 else None)
                        if i == CH - 1:
                            b.wtok = mmtok
                wdone(par, mmtok)
                last = None
                for st in range(4):
                    b = banks[st]
                    last = S.emit(
                        "scalar",
                        lambda e, b=b, j=j, st=st: e.activation(
                            out=sstage[j % 2][:, st * SW:(st + 1) * SW],
                            in_=b.ap[:], func=AF.Identity),
                        [b.wtok, sst_cons[j % 2]] + bar34, inc="CA")
                    b.rtok = last
                sst_cons[j % 2] = S.dma("sync", ST[j % 2], s_sc[j, :, :],
                                        sstage[j % 2][:], waits=[last])
            s_store_done = [("STA", S.cnt["STA"]), ("STB", S.cnt["STB"])]

        with contextlib.ExitStack() as c5:
            E5 = c5.enter_context
            bar45 = S.bar()
            xn2 = E5(nc.sbuf_tensor("xn2", [128, CH * N], BF16))
            h_sb = E5(nc.sbuf_tensor("h_sb", [128, JI * 1024], BF16))
            sg = [E5(nc.sbuf_tensor(f"sg{i}", [128, 1024], F32)) for i in range(2)]
            gg = [E5(nc.sbuf_tensor(f"gg{i}", [128, SW], F32)) for i in range(2)]
            sx = [E5(nc.sbuf_tensor(f"sx{i}", [128, 1024], F32)) for i in range(2)]
            outc = [E5(nc.sbuf_tensor(f"outc{i}", [128, SW], F32))
                    for i in range(2)]
            scl = E5(nc.sbuf_tensor("scl", [128, 2 * JI + CH], F32))
            gs_sb = scl[:, 0:JI]
            us_sb = scl[:, JI:2 * JI]
            ds_sb = scl[:, 2 * JI:2 * JI + CH]
            t_s1 = S.dma("sync", "LDX", gs_sb, gs[:, :], waits=bar45 + s_store_done)
            t_s2 = S.dma("sync", "LDX", us_sb, us[:, :])
            t_s3 = S.dma("sync", "LDX", ds_sb, ds[:, :])
            pg = [[Bank(E5(nc.psum_tensor(f"pg{g}_{s}", [128, SW], F32)))
                   for s in range(2)] for g in range(2)]
            pu = [[Bank(E5(nc.psum_tensor(f"pu{g}_{s}", [128, SW], F32)))
                   for s in range(2)] for g in range(2)]
            lt = None
            for i in range(CH):
                lt = S.dma("sync", "LDX", xn2[:, i * N:(i + 1) * N],
                           xn_sc[i, :, :],
                           waits=(xn_spill + bar45) if i == 0 else [])
            xn_ld = lt
            sg_cons = [None, None]
            gg_cons = [None, None]
            sx_cons = [None, None]
            out_st = [None, None]
            fi = [0]
            sx_pref = [None]

            def emit_sx_load(n, j, st_idx, first):
                lpar = n % 2
                w = [sx_cons[lpar]]
                if first:
                    w = w + s_store_done + bar45
                S.dma("sync", LD[lpar], sx[lpar][:, 0:SW],
                      s_sc[j, :, st_idx * SW:(st_idx + 1) * SW], waits=w)
                sx_pref[0] = S.dma("sync", LD[lpar], sx[lpar][:, SW:2 * SW],
                                   xT[j * 128:(j + 1) * 128,
                                      st_idx * SW:(st_idx + 1) * SW], waits=[])
                return sx_pref[0]

            ld_tok_cur = [None]
            for half in range(2):
                hbase = half * 1024
                h_done = None
                for j in range(JI):
                    par, wt1 = wload(gw[j], (0, D), CH)
                    wtok = S.dma("gpsimd", WS[par],
                                 wslot[par][:, D:2 * D].rearrange(
                                     "p (i n) -> p i n", i=CH),
                                 uw[j], waits=[])
                    gb = pg[j % 2]
                    ub = pu[j % 2]
                    mmt = None
                    for i in range(CH):
                        for s2 in range(2):
                            b = gb[s2]
                            w = ([wtok, xn_ld] + bar45) if (i == 0 and s2 == 0) else []
                            if i == 0:
                                w = w + [b.rtok]
                            t = S.emit(
                                "tensor",
                                lambda e, b=b, par=par, i=i, s2=s2, hb=hbase,
                                       s0=(i == 0), s1=(i == CH - 1):
                                    e.matmul(
                                        b.ap[:],
                                        wslot[par][:, i * 128:(i + 1) * 128],
                                        xn2[:, i * N + hb + s2 * SW:
                                            i * N + hb + s2 * SW + SW],
                                        start=s0, stop=s1),
                                w, inc="CP" if i == CH - 1 else None)
                            if i == CH - 1:
                                b.wtok = t
                    for i in range(CH):
                        for s2 in range(2):
                            b = ub[s2]
                            w = [b.rtok] if i == 0 else []
                            mmt = S.emit(
                                "tensor",
                                lambda e, b=b, par=par, i=i, s2=s2, hb=hbase,
                                       s0=(i == 0), s1=(i == CH - 1):
                                    e.matmul(
                                        b.ap[:],
                                        wslot[par][:, D + i * 128:
                                                   D + (i + 1) * 128],
                                        xn2[:, i * N + hb + s2 * SW:
                                            i * N + hb + s2 * SW + SW],
                                        start=s0, stop=s1),
                                w, inc="CP" if i == CH - 1 else None)
                            if i == CH - 1:
                                b.wtok = mmt
                    wdone(par, mmt)
                    for s2 in range(2):
                        b = gb[s2]
                        t_sg = S.emit(
                            "scalar",
                            lambda e, b=b, j=j, s2=s2, par2=(j % 2): e.activation(
                                out=sg[par2][:, s2 * SW:(s2 + 1) * SW],
                                in_=b.ap[:], func=AF.Silu,
                                scale=gs_sb[:, j:j + 1]),
                            [b.wtok, xn_ld, sg_cons[j % 2]] + bar45, inc="CA")
                        b.rtok = t_sg
                        bu = ub[s2]
                        t_h = S.emit(
                            "vector",
                            lambda e, bu=bu, j=j, s2=s2, par2=(j % 2):
                                e.scalar_tensor_tensor(
                                    h_sb[:, j * 1024 + s2 * SW:
                                         j * 1024 + s2 * SW + SW],
                                    bu.ap[:], us_sb[:, j:j + 1],
                                    sg[par2][:, s2 * SW:(s2 + 1) * SW],
                                    ALU.mult, ALU.mult),
                            [bu.wtok, t_sg, xn_ld] + bar45, inc="CD")
                        bu.rtok = t_h
                        h_done = t_h
                    sg_cons[j % 2] = h_done
                for j in range(CH):
                    par, wtok = wload(dw[j], (0, NI), JI)
                    db = pg[j % 2]
                    mmt = None
                    for i in range(JI):
                        for s2 in range(2):
                            b = db[s2]
                            w = [wtok, h_done] if (i == 0 and s2 == 0) else []
                            if i == 0:
                                w = w + [b.rtok]
                            mmt = S.emit(
                                "tensor",
                                lambda e, b=b, par=par, i=i, s2=s2,
                                       s0=(i == 0), s1=(i == JI - 1):
                                    e.matmul(
                                        b.ap[:],
                                        wslot[par][:, i * 128:(i + 1) * 128],
                                        h_sb[:, i * 1024 + s2 * SW:
                                             i * 1024 + s2 * SW + SW],
                                        start=s0, stop=s1),
                                w, inc="CP" if i == JI - 1 else None)
                            if i == JI - 1:
                                b.wtok = mmt
                    wdone(par, mmt)
                    for s2 in range(2):
                        st_idx = 2 * half + s2
                        n = fi[0]
                        lpar = n % 2
                        if n == 0:
                            ld_tok_cur[0] = emit_sx_load(0, j, st_idx, True)
                        ld_tok = ld_tok_cur[0]
                        if not (half == 1 and j == CH - 1 and s2 == 1):
                            jn, s2n, halfn = j, s2 + 1, half
                            if s2n == 2:
                                s2n = 0
                                jn += 1
                            if jn == CH:
                                jn = 0
                                halfn += 1
                            ld_tok_cur[0] = emit_sx_load(
                                n + 1, jn, 2 * halfn + s2n, False)
                        b = db[s2]
                        t_gg = S.emit(
                            "scalar",
                            lambda e, b=b, j=j, par2=(j % 2): e.activation(
                                out=gg[par2][:], in_=b.ap[:], func=AF.Silu,
                                scale=ds_sb[:, j:j + 1]),
                            [b.wtok, xn_ld, gg_cons[j % 2]], inc="CA")
                        b.rtok = t_gg
                        t_f1 = S.emit(
                            "vector",
                            lambda e, par2=(j % 2), lpar=lpar: e.tensor_mul(
                                outc[lpar][:], gg[par2][:], sx[lpar][:, 0:SW]),
                            [t_gg, ld_tok, out_st[lpar]], inc="CD")
                        gg_cons[j % 2] = t_f1
                        t_f2 = S.emit(
                            "vector",
                            lambda e, lpar=lpar: e.tensor_add(
                                outc[lpar][:], outc[lpar][:],
                                sx[lpar][:, SW:2 * SW]),
                            [], inc="CD")
                        sx_cons[lpar] = t_f2
                        out_st[lpar] = S.dma(
                            "sync", ST[lpar],
                            outT[j * 128:(j + 1) * 128,
                                 st_idx * SW:(st_idx + 1) * SW],
                            outc[lpar][:], waits=[t_f2])
                        fi[0] += 1

        S.emit("sync", lambda e: e.nop(),
               [("STA", S.cnt["STA"]), ("STB", S.cnt["STB"])])

        sems = {name: E(nc.semaphore(name)) for name in S.names}
        block = E(nc.Block())

        def mk(ename):
            def run(eng):
                for w, fn, inc, n in S.q[ename]:
                    for s, v in w[:-1]:
                        eng.wait_ge(sems[s], v)
                    ins = fn(eng)
                    if w:
                        s, v = w[-1]
                        ins._wait_ge(sems[s], v)
                    if inc is not None:
                        ins.then_inc(sems[inc], n)
            return run

        block.sync(mk("sync"))
        block.gpsimd(mk("gpsimd"))
        block.scalar(mk("scalar"))
        block.vector(mk("vector"))
        block.tensor(mk("tensor"))
    return nc


def _tile4(w, jn, inn):
    j, n = jn
    i, p = inn
    return np.ascontiguousarray(
        np.asarray(w, dtype=np.float32).reshape(j, n, i, p).transpose(0, 3, 2, 1))


def _consts():
    half = 64
    try:
        import jax
        import jax.numpy as jnp
        with jax.default_device(jax.devices("cpu")[0]):
            tj = 10000.0 ** (-jnp.arange(half, dtype=jnp.float32) * 2.0 / 128)
            aj = (jnp.arange(N, dtype=jnp.float32)[:, None] * tj[None, :])
            cosv = np.asarray(jnp.cos(aj)).T
            sinv = np.asarray(jnp.sin(aj)).T
        cossin = np.concatenate([cosv, sinv], axis=0).astype(np.float32)
    except Exception:
        theta = (10000.0 ** (-np.arange(half, dtype=np.float64) * 2.0 / 128.0))
        ang = theta[:, None] * np.arange(N, dtype=np.float64)[None, :]
        cossin = np.concatenate([np.cos(ang), np.sin(ang)],
                                axis=0).astype(np.float32)
    maskv = (np.arange(128)[:, None] <= np.arange(128)[None, :]).astype(np.float32)
    return cossin, np.ascontiguousarray(maskv)


def kernel(L0_attn_x, L0_attn_Wq, L0_attn_Wk, L0_attn_Wv, L0_attn_Wo,
           L0_ffn_gate_w, L0_ffn_up_w, L0_ffn_down_w, L0_ffn_gate_s,
           L0_ffn_up_s, L0_ffn_down_s):
    x = np.asarray(L0_attn_x, dtype=np.float32)
    B = x.shape[0]
    cossin, maskv = _consts()
    shared = {
        "wq_hl": _tile4(L0_attn_Wq, (CH, 128), (CH, 128)),
        "wk_hl": _tile4(L0_attn_Wk, (CH, 128), (CH, 128)),
        "wv_hl": _tile4(L0_attn_Wv, (CH, 128), (CH, 128)),
        "wo_hl": _tile4(L0_attn_Wo, (CH, 128), (CH, 128)),
        "gate_hl": _tile4(L0_ffn_gate_w, (JI, 128), (CH, 128)),
        "up_hl": _tile4(L0_ffn_up_w, (JI, 128), (CH, 128)),
        "down_hl": _tile4(L0_ffn_down_w, (CH, 128), (JI, 128)),
        "gate_s_t": np.ascontiguousarray(
            np.asarray(L0_ffn_gate_s, np.float32).reshape(JI, 128).T),
        "up_s_t": np.ascontiguousarray(
            np.asarray(L0_ffn_up_s, np.float32).reshape(JI, 128).T),
        "down_s_t": np.ascontiguousarray(
            np.asarray(L0_ffn_down_s, np.float32).reshape(CH, 128).T),
        "cossin": cossin,
        "maskv": maskv,
    }
    in_maps = []
    for c in range(B):
        m = dict(shared)
        m["xT"] = np.ascontiguousarray(x[c].T)
        in_maps.append(m)
    nc = build_nc(debug=False)
    res = run_bass_kernel_spmd(nc, in_maps, core_ids=list(range(B)))
    out = np.stack([np.asarray(res.results[c]["outT"]).T for c in range(B)],
                   axis=0)
    return np.ascontiguousarray(out.astype(np.float32))


# version 19
# speedup vs baseline: 1.0559x; 1.0082x over previous
import contextlib
import numpy as np
import ml_dtypes
import concourse.bass as bass
import concourse.mybir as mybir
from concourse.bass_utils import run_bass_kernel_spmd

F32 = mybir.dt.float32
BF16 = mybir.dt.bfloat16
AF = mybir.ActivationFunctionType
ALU = mybir.AluOpType

D = 2048
N = 2048
CH = D // 128
H = 16
NI = 5632
JI = NI // 128
NSTRIP = 4
SW = 512
KT = N // 128
EPS_LN = 1e-5
EPS_ATT = 1e-6

ENGS = ("sync", "gpsimd", "scalar", "vector", "tensor")


class Sched:

    def __init__(self):
        self.q = {e: [] for e in ENGS}
        self.cnt = {}
        self.waited = {e: {} for e in ENGS}
        self.names = []
        self.last_comp = {e: None for e in ENGS}

    def sem(self, name):
        if name not in self.cnt:
            self.cnt[name] = 0
            self.names.append(name)
        return name

    def emit(self, eng, fn, waits=(), inc=None, n=1):
        waits = list(waits)
        if eng != "tensor" and (inc in ("CP", "CA", "CD", "CG") or inc is None):
            waits.append(self.last_comp[eng])
        w = []
        for t in waits:
            if t is None:
                continue
            s, v = t
            assert v <= self.cnt[s], ("wait-on-future", s, v, self.cnt[s])
            if v > self.waited[eng].get(s, 0):
                self.waited[eng][s] = v
                w.append((s, v))
        tok = None
        if inc is not None:
            self.cnt[inc] += n
            tok = (inc, self.cnt[inc])
            if inc in ("CP", "CA", "CD", "CG"):
                self.last_comp[eng] = tok
        self.q[eng].append((w, fn, inc, n))
        return tok

    def dma(self, eng, sem, out, in_, waits=(), transpose=False):
        if transpose:
            fn = lambda e, o=out, i=in_: e.dma_start(out=o, in_=i, transpose=True)
        else:
            fn = lambda e, o=out, i=in_: e.dma_start(out=o, in_=i)
        return self.emit(eng, fn, waits, inc=sem, n=16)

    def bar(self):
        return [("CP", self.cnt["CP"]), ("CA", self.cnt["CA"]),
                ("CD", self.cnt["CD"]), ("CG", self.cnt["CG"])]


class Bank:
    __slots__ = ("ap", "wtok", "rtok")

    def __init__(self, ap):
        self.ap = ap
        self.wtok = None
        self.rtok = None


def build_nc(debug=False):
    nc = bass.Bass("TRN2", target_bir_lowering=False)
    S = Sched()
    for s in ("LDA", "LDB", "LDX", "STA", "STB", "W0", "W1", "VT0", "VT1",
              "CP", "CA", "CD", "CG"):
        S.sem(s)
    LD = ("LDA", "LDB")
    ST = ("STA", "STB")
    WS = ("W0", "W1")
    VT = ("VT0", "VT1")

    xT = nc.dram_tensor("xT", [D, N], F32, kind="ExternalInput").ap()
    wq = nc.dram_tensor("wq_hl", [CH, 128, CH, 128], F32, kind="ExternalInput").ap()
    wk = nc.dram_tensor("wk_hl", [CH, 128, CH, 128], F32, kind="ExternalInput").ap()
    wv = nc.dram_tensor("wv_hl", [CH, 128, CH, 128], F32, kind="ExternalInput").ap()
    wo = nc.dram_tensor("wo_hl", [CH, 128, CH, 128], F32, kind="ExternalInput").ap()
    gw = nc.dram_tensor("gate_hl", [JI, 128, CH, 128], F32, kind="ExternalInput").ap()
    uw = nc.dram_tensor("up_hl", [JI, 128, CH, 128], F32, kind="ExternalInput").ap()
    dw = nc.dram_tensor("down_hl", [CH, 128, JI, 128], F32, kind="ExternalInput").ap()
    gs = nc.dram_tensor("gate_s_t", [128, JI], F32, kind="ExternalInput").ap()
    us = nc.dram_tensor("up_s_t", [128, JI], F32, kind="ExternalInput").ap()
    ds = nc.dram_tensor("down_s_t", [128, CH], F32, kind="ExternalInput").ap()
    cs = nc.dram_tensor("cossin", [128, N], F32, kind="ExternalInput").ap()
    mk = nc.dram_tensor("maskv", [128, 128], F32, kind="ExternalInput").ap()
    outT = nc.dram_tensor("outT", [D, N], F32, kind="ExternalOutput").ap()

    ksc = "ExternalOutput" if debug else "Internal"
    xn_sc = nc.dram_tensor("xn_sc", [CH, 128, N], BF16, kind=ksc).ap()
    q_sc = nc.dram_tensor("q_sc", [CH, 128, N], BF16, kind=ksc).ap()
    v_sc = nc.dram_tensor("v_sc", [D, N], BF16, kind=ksc).ap()
    o_sc = nc.dram_tensor("o_sc", [CH, 128, N], BF16, kind=ksc).ap()
    s_sc = nc.dram_tensor("s_sc", [CH, 128, N], F32, kind=ksc).ap()

    xT_r = xT.rearrange("(i p) t -> i p t", p=128)
    xn_sc_r = xn_sc.rearrange("i p t -> p i t")

    with contextlib.ExitStack() as ctx:
        E = ctx.enter_context
        wslot = [E(nc.sbuf_tensor(f"wslot{i}", [128, NI], BF16)) for i in range(2)]
        ones_f = E(nc.sbuf_tensor("ones_f", [128, 1], F32))
        ones_b = E(nc.sbuf_tensor("ones_b", [128, 1], BF16))
        onesM = E(nc.sbuf_tensor("onesM", [128, 128], BF16))
        negone = E(nc.sbuf_tensor("negone", [128, 1], F32))
        t_ones = S.emit("vector", lambda e: (e.memset(ones_f[:], 1.0),
                                             e.memset(ones_b[:], 1.0),
                                             e.memset(onesM[:], 1.0),
                                             e.memset(negone[:], -1.0))[-1],
                        inc="CD")
        wcons = [None, None]
        wc = [0]

        def wload(dram_ap, cols, ncols_i):
            par = wc[0] % 2
            view = wslot[par][:, cols[0]:cols[1]].rearrange(
                "p (i n) -> p i n", i=ncols_i)
            tok = S.dma("gpsimd", WS[par], view, dram_ap, waits=[wcons[par]])
            return par, tok

        def wdone(par, tok):
            wcons[par] = tok
            wc[0] += 1

        xn_ready = []
        xn_spill = []
        k_ready = []

        with contextlib.ExitStack() as c13:
            E13 = c13.enter_context
            K_sb = E13(nc.sbuf_tensor("K_sb", [128, H * N], BF16))
            with contextlib.ExitStack() as c12:
                E12 = c12.enter_context
                cossin = E12(nc.sbuf_tensor("cossin_sb", [128, N], F32))
                ones_r = E12(nc.sbuf_tensor("ones_r", [1, 128], F32))
                eps_sb = E12(nc.sbuf_tensor("eps_sb", [1, 1], F32))
                t_ones2 = S.emit("vector",
                                 lambda e: (e.memset(ones_r[:], 1.0),
                                            e.memset(eps_sb[:], EPS_LN))[-1],
                                 inc="CD")
                t_cs = S.dma("sync", "LDX", cossin[:], cs[:, :])
                xn_sb = E12(nc.sbuf_tensor("xn_sb", [128, CH * N], BF16))

                with contextlib.ExitStack() as c1:
                    E1 = c1.enter_context
                    xa = [E1(nc.sbuf_tensor(f"xa{i}", [128, 4 * SW], F32))
                          for i in range(2)]
                    xb = [E1(nc.sbuf_tensor(f"xb{i}", [128, 4 * SW], F32))
                          for i in range(2)]
                    sq = [E1(nc.sbuf_tensor(f"sq{i}", [128, SW], F32))
                          for i in range(2)]
                    tb = [E1(nc.sbuf_tensor(f"tb{i}", [128, SW], F32))
                          for i in range(2)]
                    stat = [E1(nc.sbuf_tensor("stat0", [1, 4 * SW], F32))] * 2
                    ps_sum = [Bank(E1(nc.psum_tensor(f"ps_sum{i}", [1, SW], F32)))
                              for i in range(2)]
                    ps_sq = [Bank(E1(nc.psum_tensor(f"ps_sq{i}", [1, SW], F32)))
                             for i in range(2)]
                    ps_mb = [Bank(E1(nc.psum_tensor(f"ps_mb{i}", [128, SW], F32)))
                             for i in range(2)]
                    ps_rb = [Bank(E1(nc.psum_tensor(f"ps_rb{i}", [128, SW], F32)))
                             for i in range(2)]
                    xT_g = xT.rearrange("(g c p) t -> g p c t", g=4, p=128)

                    xa_cons = [None, None]
                    xa_cons_a = [None, None]
                    xb_cons = [None, None]
                    sq_cons = [None, None]
                    tb_cons = [None, None]
                    rb_tok = [None, None]
                    pe_stats_tok = [None, None]

                    def p1_pass1(s):
                        sp = s % 2
                        pss, psq = ps_sum[sp], ps_sq[sp]
                        for g in range(4):
                            p = g % 2
                            lt = S.dma(
                                "sync", LD[p],
                                xa[p][:].rearrange("q (c t) -> q c t", c=4),
                                xT_g[g][:, :, s * SW:(s + 1) * SW],
                                waits=[xa_cons[p], xa_cons_a[p]])
                            for c in range(4):
                                i = 4 * g + c
                                pq = i % 2
                                xv = xa[p][:, c * SW:(c + 1) * SW]
                                tsq = S.emit(
                                    "scalar",
                                    lambda e, pq=pq, xv=xv: e.activation(
                                        out=sq[pq][:], in_=xv, func=AF.Square),
                                    [lt, sq_cons[pq]], inc="CA")
                                last = S.emit(
                                    "tensor",
                                    lambda e, xv=xv, pss=pss,
                                           s0=(i == 0), s1=(i == CH - 1):
                                        e.matmul(pss.ap[:], ones_f[:], xv,
                                                 start=s0, stop=s1),
                                    [lt, t_ones] +
                                    ([pss.rtok, psq.rtok] if i == 0 else []),
                                    inc="CP")
                                if c == 3:
                                    xa_cons[p] = last
                                    xa_cons_a[p] = tsq
                                sq_cons[pq] = S.emit(
                                    "tensor",
                                    lambda e, pq=pq, psq=psq,
                                           s0=(i == 0), s1=(i == CH - 1):
                                        e.matmul(psq.ap[:], ones_f[:], sq[pq][:],
                                                 start=s0, stop=s1),
                                    [tsq], inc="CP")
                        pe_stats_tok[sp] = sq_cons[(CH - 1) % 2]

                    def p1_stats(s):
                        sp = s % 2
                        pss, psq = ps_sum[sp], ps_sq[sp]
                        mean = stat[sp][0:1, 0:SW]
                        ex2 = stat[sp][0:1, SW:2 * SW]
                        m2 = stat[sp][0:1, 2 * SW:3 * SW]
                        rstd = stat[sp][0:1, 3 * SW:4 * SW]
                        t_mean = S.emit(
                            "vector",
                            lambda e, mean=mean, pss=pss: e.tensor_scalar_mul(
                                mean, pss.ap[:], 1.0 / D),
                            [pe_stats_tok[sp]], inc="CD")
                        pss.rtok = t_mean
                        t_ex2 = S.emit(
                            "vector",
                            lambda e, ex2=ex2, psq=psq: e.tensor_scalar_mul(
                                ex2, psq.ap[:], 1.0 / D),
                            [], inc="CD")
                        psq.rtok = t_ex2
                        S.emit("vector",
                               lambda e, m2=m2, mean=mean: e.tensor_mul(
                                   m2, mean, mean), [], inc="CD")
                        t_var = S.emit("vector",
                                       lambda e, ex2=ex2, m2=m2: e.tensor_sub(
                                           ex2, ex2, m2), [], inc="CD")
                        t_sd = S.emit(
                            "scalar",
                            lambda e, m2=m2, ex2=ex2: e.activation(
                                out=m2, in_=ex2, func=AF.Sqrt,
                                bias=eps_sb[0:1, 0:1]),
                            [t_var, t_ones2], inc="CA")
                        t_rstd = S.emit("vector",
                                        lambda e, rstd=rstd, m2=m2:
                                            e.reciprocal(rstd, m2),
                                        [t_sd], inc="CD")
                        t_mb = S.emit(
                            "tensor",
                            lambda e, sp=sp, mean=mean: e.matmul(
                                ps_mb[sp].ap[:], ones_r[:], mean,
                                start=True, stop=True),
                            [t_mean, t_ones2, ps_mb[sp].rtok], inc="CP")
                        t_rb = S.emit(
                            "tensor",
                            lambda e, sp=sp, rstd=rstd: e.matmul(
                                ps_rb[sp].ap[:], ones_r[:], rstd,
                                start=True, stop=True),
                            [t_rstd, ps_rb[sp].rtok], inc="CP")
                        rb_tok[sp] = (t_mb, t_rb)

                    def p1_pass2(s):
                        sp = s % 2
                        t_mb, t_rb = rb_tok[sp]
                        t_norm = None
                        for g in range(4):
                            p = g % 2
                            lt = S.dma(
                                "sync", VT[p],
                                xb[p][:].rearrange("q (c t) -> q c t", c=4),
                                xT_g[g][:, :, s * SW:(s + 1) * SW],
                                waits=[xb_cons[p]])
                            for c in range(4):
                                i = 4 * g + c
                                pq = i % 2
                                xv = xb[p][:, c * SW:(c + 1) * SW]
                                S.emit(
                                    "vector",
                                    lambda e, pq=pq, xv=xv, sp=sp: e.tensor_sub(
                                        tb[pq][:], xv, ps_mb[sp].ap[:]),
                                    [lt, t_mb, tb_cons[pq]], inc="CD")
                                t_norm = S.emit(
                                    "vector",
                                    lambda e, pq=pq, i=i, s=s, sp=sp: e.tensor_mul(
                                        xn_sb[:, i * N + s * SW:
                                              i * N + (s + 1) * SW],
                                        tb[pq][:], ps_rb[sp].ap[:]),
                                    [t_rb], inc="CD")
                                tb_cons[pq] = t_norm
                                if c == 3:
                                    xb_cons[p] = t_norm
                        ps_mb[sp].rtok = t_norm
                        ps_rb[sp].rtok = t_norm
                        xn_ready.append(t_norm)
                        xn_view = xn_sb[:].rearrange("p (i t) -> p i t", i=CH)
                        xn_spill.append(S.dma(
                            "sync", ST[s % 2],
                            xn_sc_r[:, :, s * SW:(s + 1) * SW],
                            xn_view[:, :, s * SW:(s + 1) * SW],
                            waits=[t_norm]))

                    for s in range(NSTRIP + 1):
                        if s < NSTRIP:
                            p1_pass1(s)
                            p1_stats(s)
                        if s >= 1:
                            p1_pass2(s - 1)

                with contextlib.ExitStack() as c2:
                    E2 = c2.enter_context
                    bar12 = S.bar()
                    qstage = [E2(nc.sbuf_tensor(f"qstage{i}", [128, N], BF16))
                              for i in range(2)]
                    rA1 = [E2(nc.sbuf_tensor(f"rA1_{i}", [64, SW], F32))
                           for i in range(2)]
                    rA2 = [E2(nc.sbuf_tensor(f"rA2_{i}", [64, SW], F32))
                           for i in range(2)]
                    rB1 = [E2(nc.sbuf_tensor(f"rB1_{i}", [64, SW], F32))
                           for i in range(2)]
                    rB2 = [E2(nc.sbuf_tensor(f"rB2_{i}", [64, SW], F32))
                           for i in range(2)]
                    pool_cons = [None, None]
                    grp = [[Bank(E2(nc.psum_tensor(f"pj{g}_{s}", [128, SW], F32)))
                            for s in range(4)] for g in range(2)]
                    stage_cons = [None, None]

                    def proj(wap, dest):
                        last_evac = None
                        for j in range(CH):
                            par, wtok = wload(wap[j], (0, D), CH)
                            banks = grp[j % 2]
                            mmtok = None
                            for i in range(CH):
                                for st in range(4):
                                    b = banks[st]
                                    w = ([wtok] + xn_ready + bar12
                                         if (i == 0 and st == 0) else [])
                                    if i == 0:
                                        w = w + [b.rtok]
                                    mmtok = S.emit(
                                        "tensor",
                                        lambda e, b=b, par=par, i=i, st=st,
                                               s0=(i == 0), s1=(i == CH - 1):
                                            e.matmul(
                                                b.ap[:],
                                                wslot[par][:, i * 128:(i + 1) * 128],
                                                xn_sb[:, i * N + st * SW:
                                                      i * N + (st + 1) * SW],
                                                start=s0, stop=s1),
                                        w, inc="CP" if i == CH - 1 else None)
                                    if i == CH - 1:
                                        b.wtok = mmtok
                            wdone(par, mmtok)
                            for st in range(4):
                                b = banks[st]
                                if dest == "v":
                                    tok = S.emit(
                                        "scalar",
                                        lambda e, b=b, j=j, st=st: e.activation(
                                            out=qstage[j % 2][:,
                                                              st * SW:(st + 1) * SW],
                                            in_=b.ap[:], func=AF.Identity),
                                        [b.wtok, stage_cons[j % 2]] + bar12,
                                        inc="CA")
                                    b.rtok = tok
                                    last_evac = tok
                                else:
                                    co = cossin[0:64, st * SW:(st + 1) * SW]
                                    si = cossin[64:128, st * SW:(st + 1) * SW]
                                    if dest == "k":
                                        d0 = K_sb[0:64,
                                                  j * N + st * SW: j * N + (st + 1) * SW]
                                        d1 = K_sb[64:128,
                                                  j * N + st * SW: j * N + (st + 1) * SW]
                                        war = []
                                    else:
                                        d0 = qstage[j % 2][0:64, st * SW:(st + 1) * SW]
                                        d1 = qstage[j % 2][64:128, st * SW:(st + 1) * SW]
                                        war = [stage_cons[j % 2]]
                                    rp = (j * 4 + st) % 2
                                    S.emit(
                                        "vector",
                                        lambda e, b=b, co=co, rp=rp: e.tensor_mul(
                                            rA1[rp][:], b.ap[0:64, :], co),
                                        [b.wtok, t_cs] + war + bar12, inc="CD")
                                    S.emit(
                                        "vector",
                                        lambda e, b=b, si=si, rp=rp: e.tensor_mul(
                                            rA2[rp][:], b.ap[64:128, :], si),
                                        [], inc="CD")
                                    S.emit(
                                        "vector",
                                        lambda e, d0=d0, rp=rp: e.tensor_sub(
                                            d0, rA1[rp][:], rA2[rp][:]),
                                        [], inc="CD")
                                    S.emit(
                                        "vector",
                                        lambda e, b=b, si=si, rp=rp: e.tensor_mul(
                                            rB1[rp][:], b.ap[0:64, :], si),
                                        [], inc="CD")
                                    tok = S.emit(
                                        "vector",
                                        lambda e, b=b, co=co, rp=rp: e.tensor_mul(
                                            rB2[rp][:], b.ap[64:128, :], co),
                                        [], inc="CD")
                                    b.rtok = tok
                                    last_evac = S.emit(
                                        "vector",
                                        lambda e, d1=d1, rp=rp: e.tensor_add(
                                            d1, rB1[rp][:], rB2[rp][:]),
                                        [], inc="CD")
                            if dest == "q":
                                stage_cons[j % 2] = S.dma(
                                    "sync", ST[j % 2], q_sc[j, :, :],
                                    qstage[j % 2][:], waits=[last_evac])
                            elif dest == "v":
                                stage_cons[j % 2] = S.dma(
                                    "sync", ST[j % 2],
                                    v_sc[j * 128:(j + 1) * 128, :],
                                    qstage[j % 2][:], waits=[last_evac])
                            else:
                                k_ready.append(last_evac)
                        return last_evac

                    proj(wq, "q")
                    proj(wk, "k")
                    v_last = proj(wv, "v")
                    qv_store_done = [("STA", S.cnt["STA"]), ("STB", S.cnt["STB"])]

            with contextlib.ExitStack() as c3:
                E3 = c3.enter_context
                bar23 = S.bar()
                masks = E3(nc.sbuf_tensor("masks", [128, 128], F32))
                t_mk = S.dma("sync", "LDX", masks[:], mk[:, :],
                             waits=bar23 + qv_store_done)
                Qs = [E3(nc.sbuf_tensor(f"Qs{i}", [128, H * SW], BF16))
                      for i in range(2)]
                V_sb = E3(nc.sbuf_tensor("V_sb", [128, KT * D], BF16))
                S4 = [[E3(nc.sbuf_tensor(f"S4_{a}_{b}", [128, SW], BF16))
                       for b in range(2)] for a in range(2)]
                T1 = [[E3(nc.sbuf_tensor(f"T1_{a}_{b}", [128, SW], F32))
                       for b in range(2)] for a in range(2)]
                rc1 = [E3(nc.sbuf_tensor(f"rc1_{i}", [128, SW], F32))
                       for i in range(2)]
                Obf = [E3(nc.sbuf_tensor(f"Obf{i}", [128, SW], BF16))
                       for i in range(2)]
                ps_s = [[Bank(E3(nc.psum_tensor(f"ps_s{a}_{b}", [128, SW], F32)))
                         for b in range(2)] for a in range(2)]
                ps_o = [Bank(E3(nc.psum_tensor(f"ps_o{i}", [128, SW], F32)))
                        for i in range(2)]
                ps_d = [Bank(E3(nc.psum_tensor(f"ps_d{i}", [128, SW], F32)))
                        for i in range(2)]

                kr = [k_ready[-1], v_last]
                qs_load_tok = [None] * NSTRIP
                qs_cons = [None, None]
                s4_cons = [[None] * 2 for _ in range(2)]
                t1_cons = [[None] * 2 for _ in range(2)]
                obf_st = [None, None]
                rc1_cons = [None, None]

                def load_qstrip(qs):
                    par = qs % 2
                    lt = None
                    for b in range(4):
                        lt = S.dma(
                            "sync", LD[par],
                            Qs[par][:, b * 4 * SW:(b + 1) * 4 * SW].rearrange(
                                "p (j t) -> p j t", j=4),
                            q_sc[4 * b:4 * b + 4, :,
                                 qs * SW:(qs + 1) * SW].rearrange(
                                "j p t -> p j t"),
                            waits=([qs_cons[par]] + qv_store_done + bar23)
                            if b == 0 else [])
                    qs_load_tok[qs] = lt

                def load_vtile(k):
                    S.dma("sync", VT[k % 2], V_sb[:, k * D:(k + 1) * D],
                          v_sc[:, k * 128:(k + 1) * 128],
                          waits=qv_store_done + bar23, transpose=True)

                load_qstrip(0)
                for k in range(4):
                    load_vtile(k)

                pend = []

                def emit_finalize():
                    while pend:
                        h, qs, hsel, sv_stop, dn_stop = pend.pop(0)
                        bo, bd = ps_o[hsel], ps_d[hsel]
                        hp = h % 2
                        t_dm = S.emit(
                            "vector",
                            lambda e, bd=bd, hp=hp: e.tensor_scalar_max(
                                rc1[hp][:], bd.ap[:], EPS_ATT),
                            [dn_stop, rc1_cons[hp]], inc="CD")
                        bd.rtok = t_dm
                        t_ln = S.emit(
                            "scalar",
                            lambda e, hp=hp: e.activation(
                                out=rc1[hp][:], in_=rc1[hp][:], func=AF.Ln),
                            [t_dm], inc="CA")
                        t_ex = S.emit(
                            "scalar",
                            lambda e, hp=hp: e.activation(
                                out=rc1[hp][:], in_=rc1[hp][:], func=AF.Exp,
                                scale=negone[:, 0:1]),
                            [t_ln, t_ones], inc="CA")
                        t_ob = S.emit(
                            "vector",
                            lambda e, bo=bo, hp=hp: e.tensor_mul(
                                Obf[hp][:], bo.ap[:], rc1[hp][:]),
                            [t_ex, sv_stop, obf_st[hp]], inc="CD")
                        bo.rtok = t_ob
                        rc1_cons[hp] = t_ob
                        obf_st[hp] = S.dma(
                            "sync", ST[hp], o_sc[h, :, qs * SW:(qs + 1) * SW],
                            Obf[hp][:], waits=[t_ob])

                for qs in range(NSTRIP):
                    nk = 4 * qs + 4
                    vt_full = [("VT0", S.cnt["VT0"]), ("VT1", S.cnt["VT1"])]
                    if qs + 1 < NSTRIP:
                        load_qstrip(qs + 1)
                    for g in range(H // 2):
                        hh = (2 * g, 2 * g + 1)
                        qk_tok = [[None] * nk, [None] * nk]
                        sq_tok = [[None] * nk, [None] * nk]

                        def emit_qk(hsel, k, qs=qs, qk_tok=qk_tok, hh=hh):
                            kp = k % 2
                            b = ps_s[hsel][kp]
                            h = hh[hsel]
                            o = k - 4 * qs
                            off = max(o, 0) * 128
                            w = [qs_load_tok[qs], b.rtok] + bar23
                            if g == 0 and hsel == 0 and k == 0:
                                w = w + kr
                            qk_tok[hsel][k] = S.emit(
                                "tensor",
                                lambda e, b=b, h=h, k=k, off=off: e.matmul(
                                    b.ap[:, off:SW],
                                    K_sb[:, h * N + k * 128: h * N + k * 128 + 128],
                                    Qs[qs % 2][:, h * SW + off:(h + 1) * SW],
                                    start=True, stop=True),
                                w, inc="CP")
                            b.wtok = qk_tok[hsel][k]

                        emit_qk(0, 0)
                        emit_qk(1, 0)
                        if nk > 1:
                            emit_qk(0, 1)
                            emit_qk(1, 1)
                        emit_finalize()
                        sv = [None, None]
                        dn = [None, None]
                        for k in range(nk):
                            kp = k % 2
                            o = k - 4 * qs
                            off = max(o, 0) * 128
                            for hsel in range(2):
                                bs = ps_s[hsel][kp]
                                t_sq1 = S.emit(
                                    "scalar",
                                    lambda e, bs=bs, hsel=hsel, kp=kp, off=off:
                                        e.activation(
                                            out=T1[hsel][kp][:, off:SW],
                                            in_=bs.ap[:, off:SW],
                                            func=AF.Square),
                                    [qk_tok[hsel][k], t1_cons[hsel][kp]] + bar23,
                                    inc="CA")
                                bs.rtok = t_sq1
                                dep = t_sq1
                                if o >= 0:
                                    dep = S.emit(
                                        "vector",
                                        lambda e, hsel=hsel, kp=kp, off=off:
                                            e.tensor_mul(
                                                T1[hsel][kp][:, off:off + 128],
                                                T1[hsel][kp][:, off:off + 128],
                                                masks[:]),
                                        [t_sq1, t_mk] + bar23, inc="CD")
                                sq_eng, sq_sem = (("gpsimd", "CG") if hsel
                                                  else ("vector", "CD"))
                                sq_tok[hsel][k] = S.emit(
                                    sq_eng,
                                    lambda e, hsel=hsel, kp=kp, off=off:
                                        e.tensor_mul(
                                            S4[hsel][kp][:, off:SW],
                                            T1[hsel][kp][:, off:SW],
                                            T1[hsel][kp][:, off:SW]),
                                    [dep, s4_cons[hsel][kp]] + bar23, inc=sq_sem)
                                t1_cons[hsel][kp] = sq_tok[hsel][k]
                            if k + 2 < nk:
                                emit_qk(0, k + 2)
                                emit_qk(1, k + 2)
                            for hsel in range(2):
                                bo, bd = ps_o[hsel], ps_d[hsel]
                                h = hh[hsel]
                                sv[hsel] = S.emit(
                                    "tensor",
                                    lambda e, bo=bo, hsel=hsel, kp=kp, h=h,
                                           k=k, off=off,
                                           s0=(k == 0), s1=(k == nk - 1):
                                        e.matmul(
                                            bo.ap[:, off:SW],
                                            V_sb[:, k * D + h * 128:
                                                 k * D + (h + 1) * 128],
                                            S4[hsel][kp][:, off:SW],
                                            start=s0, stop=s1),
                                    [sq_tok[hsel][k]] + vt_full +
                                    ([bo.rtok] if k == 0 else []), inc="CP")
                                dn[hsel] = S.emit(
                                    "tensor",
                                    lambda e, bd=bd, hsel=hsel, kp=kp, off=off,
                                           s0=(k == 0), s1=(k == nk - 1):
                                        e.matmul(
                                            bd.ap[:, off:SW], onesM[:],
                                            S4[hsel][kp][:, off:SW],
                                            start=s0, stop=s1),
                                    [t_ones] + ([bd.rtok] if k == 0 else []),
                                    inc="CP")
                                s4_cons[hsel][kp] = dn[hsel]
                        if g == H // 2 - 1:
                            qs_cons[qs % 2] = dn[1]
                        pend.append((hh[0], qs, 0, sv[0], dn[0]))
                        pend.append((hh[1], qs, 1, sv[1], dn[1]))
                    if qs + 1 < NSTRIP:
                        emit_finalize()
                        for kk in range(4 * qs + 4, 4 * qs + 8):
                            load_vtile(kk)
                emit_finalize()
                o_store_done = [("STA", S.cnt["STA"]), ("STB", S.cnt["STB"])]

        with contextlib.ExitStack() as c4:
            E4 = c4.enter_context
            bar34 = S.bar()
            O_sb = E4(nc.sbuf_tensor("O_sb", [128, H * N], BF16))
            sstage = [E4(nc.sbuf_tensor(f"sstage{i}", [128, N], F32))
                      for i in range(2)]
            grp4 = [[Bank(E4(nc.psum_tensor(f"p4{g}_{s}", [128, SW], F32)))
                     for s in range(4)] for g in range(2)]
            lt = None
            for j in range(CH):
                lt = S.dma("sync", "LDX", O_sb[:, j * N:(j + 1) * N],
                           o_sc[j, :, :],
                           waits=(o_store_done + bar34) if j == 0 else [])
            o_ld = lt
            sst_cons = [None, None]
            for j in range(CH):
                par, wtok = wload(wo[j], (0, D), CH)
                banks = grp4[j % 2]
                mmtok = None
                for i in range(CH):
                    for st in range(4):
                        b = banks[st]
                        w = ([wtok, o_ld] + bar34) if (i == 0 and st == 0) else []
                        if i == 0:
                            w = w + [b.rtok]
                        mmtok = S.emit(
                            "tensor",
                            lambda e, b=b, par=par, i=i, st=st,
                                   s0=(i == 0), s1=(i == CH - 1):
                                e.matmul(b.ap[:],
                                         wslot[par][:, i * 128:(i + 1) * 128],
                                         O_sb[:, i * N + st * SW:
                                              i * N + (st + 1) * SW],
                                         start=s0, stop=s1),
                            w, inc="CP" if i == CH - 1 else None)
                        if i == CH - 1:
                            b.wtok = mmtok
                wdone(par, mmtok)
                last = None
                for st in range(4):
                    b = banks[st]
                    last = S.emit(
                        "scalar",
                        lambda e, b=b, j=j, st=st: e.activation(
                            out=sstage[j % 2][:, st * SW:(st + 1) * SW],
                            in_=b.ap[:], func=AF.Identity),
                        [b.wtok, sst_cons[j % 2]] + bar34, inc="CA")
                    b.rtok = last
                sst_cons[j % 2] = S.dma("sync", ST[j % 2], s_sc[j, :, :],
                                        sstage[j % 2][:], waits=[last])
            s_store_done = [("STA", S.cnt["STA"]), ("STB", S.cnt["STB"])]

        with contextlib.ExitStack() as c5:
            E5 = c5.enter_context
            bar45 = S.bar()
            xn2 = E5(nc.sbuf_tensor("xn2", [128, CH * N], BF16))
            h_sb = E5(nc.sbuf_tensor("h_sb", [128, JI * 1024], BF16))
            sg = [E5(nc.sbuf_tensor(f"sg{i}", [128, 1024], F32)) for i in range(2)]
            gg = [E5(nc.sbuf_tensor(f"gg{i}", [128, SW], F32)) for i in range(2)]
            sx = [E5(nc.sbuf_tensor(f"sx{i}", [128, 1024], F32)) for i in range(2)]
            outc = [E5(nc.sbuf_tensor(f"outc{i}", [128, SW], F32))
                    for i in range(2)]
            scl = E5(nc.sbuf_tensor("scl", [128, 2 * JI + CH], F32))
            gs_sb = scl[:, 0:JI]
            us_sb = scl[:, JI:2 * JI]
            ds_sb = scl[:, 2 * JI:2 * JI + CH]
            t_s1 = S.dma("sync", "LDX", gs_sb, gs[:, :], waits=bar45 + s_store_done)
            t_s2 = S.dma("sync", "LDX", us_sb, us[:, :])
            t_s3 = S.dma("sync", "LDX", ds_sb, ds[:, :])
            pg = [[Bank(E5(nc.psum_tensor(f"pg{g}_{s}", [128, SW], F32)))
                   for s in range(2)] for g in range(2)]
            pu = [[Bank(E5(nc.psum_tensor(f"pu{g}_{s}", [128, SW], F32)))
                   for s in range(2)] for g in range(2)]
            lt = None
            for i in range(CH):
                lt = S.dma("sync", "LDX", xn2[:, i * N:(i + 1) * N],
                           xn_sc[i, :, :],
                           waits=(xn_spill + bar45) if i == 0 else [])
            xn_ld = lt
            sg_cons = [None, None]
            gg_cons = [None, None]
            sx_cons = [None, None]
            out_st = [None, None]
            fi = [0]
            sx_pref = [None]

            def emit_sx_load(n, j, st_idx, first):
                lpar = n % 2
                w = [sx_cons[lpar]]
                if first:
                    w = w + s_store_done + bar45
                S.dma("sync", LD[lpar], sx[lpar][:, 0:SW],
                      s_sc[j, :, st_idx * SW:(st_idx + 1) * SW], waits=w)
                sx_pref[0] = S.dma("sync", LD[lpar], sx[lpar][:, SW:2 * SW],
                                   xT[j * 128:(j + 1) * 128,
                                      st_idx * SW:(st_idx + 1) * SW], waits=[])
                return sx_pref[0]

            ld_tok_cur = [None]
            for half in range(2):
                hbase = half * 1024
                h_done = None
                for j in range(JI):
                    par, wt1 = wload(gw[j], (0, D), CH)
                    wtok = S.dma("gpsimd", WS[par],
                                 wslot[par][:, D:2 * D].rearrange(
                                     "p (i n) -> p i n", i=CH),
                                 uw[j], waits=[])
                    gb = pg[j % 2]
                    ub = pu[j % 2]
                    mmt = None
                    for i in range(CH):
                        for s2 in range(2):
                            b = gb[s2]
                            w = ([wtok, xn_ld] + bar45) if (i == 0 and s2 == 0) else []
                            if i == 0:
                                w = w + [b.rtok]
                            t = S.emit(
                                "tensor",
                                lambda e, b=b, par=par, i=i, s2=s2, hb=hbase,
                                       s0=(i == 0), s1=(i == CH - 1):
                                    e.matmul(
                                        b.ap[:],
                                        wslot[par][:, i * 128:(i + 1) * 128],
                                        xn2[:, i * N + hb + s2 * SW:
                                            i * N + hb + s2 * SW + SW],
                                        start=s0, stop=s1),
                                w, inc="CP" if i == CH - 1 else None)
                            if i == CH - 1:
                                b.wtok = t
                    for i in range(CH):
                        for s2 in range(2):
                            b = ub[s2]
                            w = [b.rtok] if i == 0 else []
                            mmt = S.emit(
                                "tensor",
                                lambda e, b=b, par=par, i=i, s2=s2, hb=hbase,
                                       s0=(i == 0), s1=(i == CH - 1):
                                    e.matmul(
                                        b.ap[:],
                                        wslot[par][:, D + i * 128:
                                                   D + (i + 1) * 128],
                                        xn2[:, i * N + hb + s2 * SW:
                                            i * N + hb + s2 * SW + SW],
                                        start=s0, stop=s1),
                                w, inc="CP" if i == CH - 1 else None)
                            if i == CH - 1:
                                b.wtok = mmt
                    wdone(par, mmt)
                    for s2 in range(2):
                        b = gb[s2]
                        t_sg = S.emit(
                            "scalar",
                            lambda e, b=b, j=j, s2=s2, par2=(j % 2): e.activation(
                                out=sg[par2][:, s2 * SW:(s2 + 1) * SW],
                                in_=b.ap[:], func=AF.Silu,
                                scale=gs_sb[:, j:j + 1]),
                            [b.wtok, xn_ld, sg_cons[j % 2]] + bar45, inc="CA")
                        b.rtok = t_sg
                        bu = ub[s2]
                        t_h = S.emit(
                            "vector",
                            lambda e, bu=bu, j=j, s2=s2, par2=(j % 2):
                                e.scalar_tensor_tensor(
                                    h_sb[:, j * 1024 + s2 * SW:
                                         j * 1024 + s2 * SW + SW],
                                    bu.ap[:], us_sb[:, j:j + 1],
                                    sg[par2][:, s2 * SW:(s2 + 1) * SW],
                                    ALU.mult, ALU.mult),
                            [bu.wtok, t_sg, xn_ld] + bar45, inc="CD")
                        bu.rtok = t_h
                        h_done = t_h
                    sg_cons[j % 2] = h_done
                for j in range(CH):
                    par, wtok = wload(dw[j], (0, NI), JI)
                    db = pg[j % 2]
                    mmt = None
                    for i in range(JI):
                        for s2 in range(2):
                            b = db[s2]
                            w = [wtok, h_done] if (i == 0 and s2 == 0) else []
                            if i == 0:
                                w = w + [b.rtok]
                            mmt = S.emit(
                                "tensor",
                                lambda e, b=b, par=par, i=i, s2=s2,
                                       s0=(i == 0), s1=(i == JI - 1):
                                    e.matmul(
                                        b.ap[:],
                                        wslot[par][:, i * 128:(i + 1) * 128],
                                        h_sb[:, i * 1024 + s2 * SW:
                                             i * 1024 + s2 * SW + SW],
                                        start=s0, stop=s1),
                                w, inc="CP" if i == JI - 1 else None)
                            if i == JI - 1:
                                b.wtok = mmt
                    wdone(par, mmt)
                    for s2 in range(2):
                        st_idx = 2 * half + s2
                        n = fi[0]
                        lpar = n % 2
                        if n == 0:
                            ld_tok_cur[0] = emit_sx_load(0, j, st_idx, True)
                        ld_tok = ld_tok_cur[0]
                        if not (half == 1 and j == CH - 1 and s2 == 1):
                            jn, s2n, halfn = j, s2 + 1, half
                            if s2n == 2:
                                s2n = 0
                                jn += 1
                            if jn == CH:
                                jn = 0
                                halfn += 1
                            ld_tok_cur[0] = emit_sx_load(
                                n + 1, jn, 2 * halfn + s2n, False)
                        b = db[s2]
                        t_gg = S.emit(
                            "scalar",
                            lambda e, b=b, j=j, par2=(j % 2): e.activation(
                                out=gg[par2][:], in_=b.ap[:], func=AF.Silu,
                                scale=ds_sb[:, j:j + 1]),
                            [b.wtok, xn_ld, gg_cons[j % 2]], inc="CA")
                        b.rtok = t_gg
                        t_f1 = S.emit(
                            "vector",
                            lambda e, par2=(j % 2), lpar=lpar: e.tensor_mul(
                                outc[lpar][:], gg[par2][:], sx[lpar][:, 0:SW]),
                            [t_gg, ld_tok, out_st[lpar]], inc="CD")
                        gg_cons[j % 2] = t_f1
                        t_f2 = S.emit(
                            "vector",
                            lambda e, lpar=lpar: e.tensor_add(
                                outc[lpar][:], outc[lpar][:],
                                sx[lpar][:, SW:2 * SW]),
                            [], inc="CD")
                        sx_cons[lpar] = t_f2
                        out_st[lpar] = S.dma(
                            "sync", ST[lpar],
                            outT[j * 128:(j + 1) * 128,
                                 st_idx * SW:(st_idx + 1) * SW],
                            outc[lpar][:], waits=[t_f2])
                        fi[0] += 1

        S.emit("sync", lambda e: e.nop(),
               [("STA", S.cnt["STA"]), ("STB", S.cnt["STB"])])

        sems = {name: E(nc.semaphore(name)) for name in S.names}
        block = E(nc.Block())

        def mk(ename):
            def run(eng):
                for w, fn, inc, n in S.q[ename]:
                    for s, v in w[:-1]:
                        eng.wait_ge(sems[s], v)
                    ins = fn(eng)
                    if w:
                        s, v = w[-1]
                        ins._wait_ge(sems[s], v)
                    if inc is not None:
                        ins.then_inc(sems[inc], n)
            return run

        block.sync(mk("sync"))
        block.gpsimd(mk("gpsimd"))
        block.scalar(mk("scalar"))
        block.vector(mk("vector"))
        block.tensor(mk("tensor"))
    return nc


def _tile4(w, jn, inn):
    j, n = jn
    i, p = inn
    return np.ascontiguousarray(
        np.asarray(w, dtype=np.float32).reshape(j, n, i, p).transpose(0, 3, 2, 1))


def _consts():
    half = 64
    try:
        import jax
        import jax.numpy as jnp
        with jax.default_device(jax.devices("cpu")[0]):
            tj = 10000.0 ** (-jnp.arange(half, dtype=jnp.float32) * 2.0 / 128)
            aj = (jnp.arange(N, dtype=jnp.float32)[:, None] * tj[None, :])
            cosv = np.asarray(jnp.cos(aj)).T
            sinv = np.asarray(jnp.sin(aj)).T
        cossin = np.concatenate([cosv, sinv], axis=0).astype(np.float32)
    except Exception:
        theta = (10000.0 ** (-np.arange(half, dtype=np.float64) * 2.0 / 128.0))
        ang = theta[:, None] * np.arange(N, dtype=np.float64)[None, :]
        cossin = np.concatenate([np.cos(ang), np.sin(ang)],
                                axis=0).astype(np.float32)
    maskv = (np.arange(128)[:, None] <= np.arange(128)[None, :]).astype(np.float32)
    return cossin, np.ascontiguousarray(maskv)


def kernel(L0_attn_x, L0_attn_Wq, L0_attn_Wk, L0_attn_Wv, L0_attn_Wo,
           L0_ffn_gate_w, L0_ffn_up_w, L0_ffn_down_w, L0_ffn_gate_s,
           L0_ffn_up_s, L0_ffn_down_s):
    x = np.asarray(L0_attn_x, dtype=np.float32)
    B = x.shape[0]
    cossin, maskv = _consts()
    shared = {
        "wq_hl": _tile4(L0_attn_Wq, (CH, 128), (CH, 128)),
        "wk_hl": _tile4(L0_attn_Wk, (CH, 128), (CH, 128)),
        "wv_hl": _tile4(L0_attn_Wv, (CH, 128), (CH, 128)),
        "wo_hl": _tile4(L0_attn_Wo, (CH, 128), (CH, 128)),
        "gate_hl": _tile4(L0_ffn_gate_w, (JI, 128), (CH, 128)),
        "up_hl": _tile4(L0_ffn_up_w, (JI, 128), (CH, 128)),
        "down_hl": _tile4(L0_ffn_down_w, (CH, 128), (JI, 128)),
        "gate_s_t": np.ascontiguousarray(
            np.asarray(L0_ffn_gate_s, np.float32).reshape(JI, 128).T),
        "up_s_t": np.ascontiguousarray(
            np.asarray(L0_ffn_up_s, np.float32).reshape(JI, 128).T),
        "down_s_t": np.ascontiguousarray(
            np.asarray(L0_ffn_down_s, np.float32).reshape(CH, 128).T),
        "cossin": cossin,
        "maskv": maskv,
    }
    in_maps = []
    for c in range(B):
        m = dict(shared)
        m["xT"] = np.ascontiguousarray(x[c].T)
        in_maps.append(m)
    nc = build_nc(debug=False)
    res = run_bass_kernel_spmd(nc, in_maps, core_ids=list(range(B)))
    out = np.stack([np.asarray(res.results[c]["outT"]).T for c in range(B)],
                   axis=0)
    return np.ascontiguousarray(out.astype(np.float32))
